# version 17
# baseline (speedup 1.0000x reference)
import contextlib
import numpy as np
import concourse.bass as bass
import concourse.mybir as mybir
from concourse.bass_utils import run_bass_kernel_spmd

F32 = mybir.dt.float32
BF16 = mybir.dt.bfloat16
AF = mybir.ActivationFunctionType
ALU = mybir.AluOpType
AX = mybir.AxisListType

P1STOP = None
P3STOP = None
D = 2048
NCH = 16
HA, FDA = 32, 64
HB, DB = 16, 128
FF = 8192
NCORES = 8
GRP = 512
TIL = 128

ENGINES = ("tensor", "vector", "scalar", "gpsimd", "sync")


class Em:

    def __init__(self, nc, stack):
        self.nc = nc
        self.stack = stack
        self.ops = {e: [] for e in ENGINES}
        self._sems = {}
        self._seen = {e: {} for e in ENGINES}

    def _sem(self, name):
        if name not in self._sems:
            h = self.stack.enter_context(self.nc.semaphore(name))
            self._sems[name] = [h, 0]
        return self._sems[name]

    def count(self, name):
        return self._sem(name)[1]

    def wait(self, eng, tok):
        if tok is None:
            return
        name, val = tok
        if val <= 0 or self._seen[eng].get(name, 0) >= val:
            return
        self._seen[eng][name] = val
        h = self._sem(name)[0]
        self.ops[eng].append(lambda E, h=h, v=val: E.wait_ge(h, v))

    def waits(self, eng, toks):
        for t in toks:
            self.wait(eng, t)

    def emit(self, eng, fn, inc=None, inc_by=1):
        if inc is None:
            self.ops[eng].append(fn)
            return None
        s = self._sem(inc)
        s[1] += inc_by
        val = s[1]

        def run(E, fn=fn, h=s[0], k=inc_by):
            fn(E).then_inc(h, k)

        self.ops[eng].append(run)
        return (inc, val)

    def dma(self, queue, dst, src, deps=(), stream=None, **kw):
        self.waits(queue, deps)
        return self.emit(queue, lambda E: E.dma_start(out=dst, in_=src, **kw),
                         inc="dq_" + (stream or queue), inc_by=16)

    def play(self):
        block = self.stack.enter_context(self.nc.Block())
        for eng in ENGINES:
            seq = self.ops[eng]
            if not seq:
                continue
            section = getattr(block, eng)

            def body(E, seq=seq):
                for fn in seq:
                    fn(E)
            section(body)



def _blocked(wT):
    K, O = wT.shape
    return np.ascontiguousarray(
        wT.reshape(K // 128, 128, O // 128, 128).transpose(2, 0, 1, 3))


def host_prepare(inp, TPC):
    f = np.float32
    x = np.asarray(inp["L0_attn_a_hidden_states"], f)
    B, L, _ = x.shape
    QC = NCORES // B
    assert L % QC == 0 and TPC == L // QC

    shared = {}
    A = lambda k: np.asarray(inp[k], f)
    shared["waq"] = _blocked(np.ascontiguousarray(A("L0_attn_a_Wq").T))
    shared["wak"] = _blocked(np.ascontiguousarray(A("L0_attn_a_Wk").T))
    shared["wav"] = np.ascontiguousarray(A("L0_attn_a_Wv").T)
    shared["wao"] = _blocked(np.ascontiguousarray(A("L0_attn_a_Wo").T))
    for nm, key in (("wfq", "L0_attn_a_Wfq"), ("wfk", "L0_attn_a_Wfk")):
        Wf = A(key)
        bd = np.zeros((NCH, 128, 128), f)
        for c in range(NCH):
            bd[c, 0:64, 0:64] = Wf[2 * c]
            bd[c, 64:128, 64:128] = Wf[2 * c + 1]
        shared[nm] = bd
    shared["wg1"] = _blocked(np.ascontiguousarray(A("L0_ffn_w_gate_up").T))
    shared["wd1"] = _blocked(np.ascontiguousarray(A("L0_ffn_w_down").T))
    for nm, key in (("wbq", "L0_attn_b_Wq"), ("wbk", "L0_attn_b_Wk"),
                    ("wbv", "L0_attn_b_Wv"), ("wbo", "L0_attn_b_Wo"),
                    ("wbig", "L0_attn_b_Wig"), ("wbog", "L0_attn_b_Wog")):
        shared[nm] = _blocked(A(key))
    shared["wbg"] = np.ascontiguousarray(A("L0_attn_b_Wg").reshape(NCH, 128, HB))
    shared["wgate"] = _blocked(A("L0_attn_b_Wgate"))
    shared["wval"] = _blocked(A("L0_attn_b_Wval"))
    shared["wout"] = _blocked(A("L0_attn_b_Wout"))
    cols = lambda v: np.ascontiguousarray(np.asarray(v, f).reshape(NCH, 128).T)
    shared["convw"] = np.ascontiguousarray(
        A("L0_attn_b_conv_w").reshape(NCH, 128, 4).transpose(1, 0, 2).reshape(128, NCH * 4))
    shared["convb"] = cols(A("L0_attn_b_conv_b"))
    shared["ln1w"] = cols(A("L0_attn_b_ln1w"))
    shared["ln1b"] = cols(A("L0_attn_b_ln1b"))
    shared["ln2w"] = cols(A("L0_attn_b_ln2w"))
    shared["ln2b"] = cols(A("L0_attn_b_ln2b"))
    shared["gnw"] = cols(A("L0_attn_b_gnw"))
    shared["gnb"] = cols(A("L0_attn_b_gnb"))
    shared["vnw"] = np.ascontiguousarray(A("L0_attn_b_vnw").reshape(128, 1))
    shared["vnb"] = np.ascontiguousarray(A("L0_attn_b_vnb").reshape(128, 1))
    shared["mnw"] = np.ascontiguousarray(A("L0_attn_b_mnw").reshape(128, 1))
    shared["mnb"] = np.ascontiguousarray(A("L0_attn_b_mnb").reshape(128, 1))
    shared["bigc"] = cols(A("L0_attn_b_big"))
    shared["bogc"] = cols(A("L0_attn_b_bog"))
    shared["bgrep"] = np.ascontiguousarray(np.broadcast_to(A("L0_attn_b_bg"), (128, HB)))
    shared["iden"] = np.eye(128, dtype=f)
    shared["epsc"] = np.ascontiguousarray(
        np.broadcast_to(np.array([1e-5, 1e-12, 1.0, 0.0], f), (128, 4)))
    shared["ones128"] = np.ones((128, 128), f)
    shared["trilT"] = np.ascontiguousarray(np.tril(np.ones((128, 128), f)).T)
    oh16 = np.zeros((16, 16, 128), f)
    for k in range(16):
        oh16[k, k, :] = 1.0
    shared["oh16"] = np.ascontiguousarray(oh16.transpose(1, 0, 2).reshape(16, 16 * 128))
    ohc = np.zeros((128, NCH * NCH), f)
    for h in range(NCH):
        ohc[:, NCH * h + h] = 1.0
    shared["ohcols"] = ohc

    cores = []
    for c in range(NCORES):
        b, q = divmod(c, QC)
        sl = x[b, q * TPC:(q + 1) * TPC, :]
        inc8 = np.zeros((128, NCORES), f)
        for r in (b * QC + np.arange(q)):
            inc8[:, r] = 1.0
        prevh = np.zeros((128, NCORES), f)
        if q > 0:
            prevh[:, c - 1] = 1.0
        cores.append({
            "x0": np.ascontiguousarray(sl.T),
            "inc8": inc8,
            "prevh": prevh,
            "haspred": np.full((128, 1), 1.0 if q > 0 else 0.0, f),
        })
    return shared, cores



def build_program(TPC, phases=("p1", "p2", "p3", "p4"), debug=()):
    NG = TPC // GRP
    NT = TPC // TIL
    CS = TPC + 4
    nc = bass.Bass("TRN2", target_bir_lowering=False)
    st = contextlib.ExitStack()

    dr = {}

    def din(name, shape, dtype=F32):
        dr[name] = nc.dram_tensor(name, list(shape), dtype, kind="ExternalInput").ap()
        return dr[name]

    def dint(name, shape, dtype=F32, **kw):
        dr[name] = nc.dram_tensor(name, list(shape), dtype, kind="Internal", **kw).ap()
        return dr[name]

    def dout(name, shape, dtype=F32):
        dr[name] = nc.dram_tensor(name, list(shape), dtype, kind="ExternalOutput").ap()
        return dr[name]

    din("x0", (D, TPC))
    dout("y", (D, TPC))
    dint("xb1", (D, TPC))
    din("waq", (NCH, NCH, 128, 128)); din("wak", (NCH, NCH, 128, 128))
    din("wav", (D, D)); din("wao", (NCH, NCH, 128, 128))
    din("wfq", (NCH, 128, 128)); din("wfk", (NCH, 128, 128))
    din("wg1", (128, NCH, 128, 128)); din("wd1", (NCH, 64, 128, 128))
    for nm in ("wbq", "wbk", "wbv", "wbo", "wbig", "wbog"):
        din(nm, (NCH, NCH, 128, 128))
    din("wbg", (NCH, 128, HB))
    din("wgate", (64, NCH, 128, 128)); din("wval", (64, NCH, 128, 128))
    din("wout", (NCH, 64, 128, 128))
    din("convw", (128, NCH * 4)); din("convb", (128, NCH))
    for nm in ("ln1w", "ln1b", "ln2w", "ln2b", "gnw", "gnb", "bigc", "bogc"):
        din(nm, (128, NCH))
    for nm in ("vnw", "vnb", "mnw", "mnb", "haspred"):
        din(nm, (128, 1))
    din("bgrep", (128, HB)); din("inc8", (128, NCORES)); din("prevh", (128, NCORES))
    din("iden", (128, 128)); din("ones128", (128, 128)); din("trilT", (128, 128))
    din("epsc", (128, 4))
    din("oh16", (16, 16 * 128))
    din("ohcols", (128, NCH * NCH))
    dint("fkt_sp", (NT, NCH, 128, 128), BF16)
    dint("vp_sp", (NT, NCH, 128, 128), BF16)
    dint("amt_sp", (NT, NCH, 128, 256), BF16)
    dint("yin_sp", (NT, NCH, 128, 128), BF16)
    dsp = dout if "sp" in debug else dint
    dsp("qn_sp", (NCH, 128, TPC), BF16)
    dsp("og_sp", (NCH, 128, TPC), BF16)
    dsp("kraw_sp", (NCH, 128, TPC), BF16)
    dsp("vraw_sp", (NCH, 128, TPC), BF16)
    dsp("igr_sp", (NCH, 128, TPC), BF16)
    dsp("gam_hbm", (HB, TPC))
    dint("cc1i", (128, 1040)); dint("cc1o", (128 * NCORES, 1040), addr_space="Shared")
    dint("cc2i", (128, NCH * 3)); dint("cc2o", (128 * NCORES, NCH * 3), addr_space="Shared")
    dint("cc3i", (128, 32)); dint("cc3o", (128 * NCORES, 32), addr_space="Shared")
    for nm in debug:
        if nm != "sp":
            dout("dbg_" + nm, (D, TPC))
    if "sp" in debug:
        dout("dbg_u", (D, TPC), BF16)
        dout("dbg_mem", (128, TPC))
        dout("dbg_z", (128, TPC), BF16)
        dout("dbg_pk", (128, 64))
        dout("dbg_memb", (128, TPC), BF16)
        dout("dbg_scr", (128, TPC), BF16)
        dout("dbg_st1", (128, 64))
        dout("dbg_st2", (128, 64))
        dout("dbg_ab", (32, 384), BF16)
    em_debug = set(debug)

    sb = {}

    def sbuf(name, shape, dtype=F32):
        sb[name] = st.enter_context(nc.sbuf_tensor("s_" + name, list(shape), dtype))
        return sb[name]

    sbuf("BIGA", (128, max(NCH * CS, 7 * 2048 + NCH * GRP)), BF16)
    sbuf("BIGB", (128, NCH * CS), BF16)
    sbuf("XCG", (128, NCH * GRP), BF16)
    sbuf("WX", (128, 4096), BF16)
    sbuf("WY", (128, 4096), BF16)
    sbuf("WZ", (128, 4096), BF16)
    sbuf("XGB", (128, 2 * GRP), BF16)
    sbuf("XSQ", (128, 2 * GRP), BF16)
    sbuf("T1", (128, GRP))
    sbuf("T2", (128, GRP))
    sbuf("GS", (128, 2 * GRP), BF16)
    sbuf("DOWN", (128, 2 * GRP))
    sbuf("MB", (128, 2 * GRP))
    xcg = sb["XCG"]
    sb["FKC"] = xcg[:, 0:512]
    sb["FKCZ"] = xcg[:, 512:1024]
    sb["QSL"] = xcg[:, 1024:1536]
    sb["SRUNB"] = xcg[:, 1536:2560]
    sb["SRUNZ"] = xcg[:, 2560:3584]
    sb["W32B"] = xcg[:, 3584:4096]
    sb["SRUN"] = xcg[:, 4096:6144].bitcast(F32)
    sb["W32"] = xcg[:, 6144:7168].bitcast(F32)
    sb["DEN"] = xcg[:, 7168:8192].bitcast(F32)
    sbuf("KSZ", (128, 16), BF16)
    sbuf("KS", (128, 16))
    sbuf("KSB", (128, 16), BF16)
    sbuf("AB3", (128, 64))
    sbuf("GAM", (128, NT * 16))
    sb["GAMF"] = sb["T2"][0:16, 0:128]
    sbuf("QNR", (128, 2048), BF16)
    sbuf("STT2", (128, 64))
    sbuf("ABFM", (32, 3 * 128), BF16)
    sbuf("AB2", (16, 12 * 128), BF16)
    sbuf("RSQ", (128, 256))
    sbuf("HALO", (128, NCH * 3))
    sb["GATH"] = sb["DOWN"]
    sbuf("iden", (128, 128)); sbuf("idenb", (128, 128), BF16)
    sbuf("trilT", (128, 128)); sbuf("ones1b", (1, 128), BF16)
    sbuf("onescb", (128, 1), BF16); sbuf("onesrf", (1, 128))
    sbuf("oh16b", (16, 16 * 128), BF16)
    sbuf("ohcb", (128, NCH * NCH), BF16)
    sbuf("convw", (128, NCH * 4)); sbuf("convb", (128, NCH))
    for nm in ("ln1w", "ln1b", "ln2w", "ln2b", "gnw", "gnb", "bigc", "bogc"):
        sbuf(nm, (128, NCH))
    for nm in ("vnw", "vnb", "mnw", "mnb", "haspred"):
        sbuf(nm, (128, 1))
    sbuf("bgrep", (128, HB)); sbuf("inc8", (128, NCORES)); sbuf("epsc", (128, 4))
    sbuf("prevh", (128, NCORES))

    ps = {}
    for i in range(4):
        ps[i] = st.enter_context(nc.psum_tensor(f"P{i}", [128, 512], F32))
    ps["S"] = st.enter_context(nc.psum_tensor("PS", [128, 1024], F32))
    ps["K"] = st.enter_context(nc.psum_tensor("PK", [128, 512], F32))
    ps["B"] = st.enter_context(nc.psum_tensor("PB", [128, 512], BF16))

    em = Em(nc, st)
    build_body(nc, em, dr, sb, ps, TPC, NG, NT, CS, phases, debug)
    em.play()
    st.close()
    return nc


def build_body(nc, em, dr, sb, ps, TPC, NG, NT, CS, phases, debug):
    EPS = 1e-5
    dcol = lambda c, t: c * CS + 3 + t

    const_toks = []
    cl = [("iden", "iden", "sync"), ("trilT", "trilT", "sync"),
          ("convw", "convw", "sync"), ("convb", "convb", "sync"),
          ("bgrep", "bgrep", "sync"), ("inc8", "inc8", "sync"),
          ("idenb", "iden", "gpsimd"),
          ("oh16b", "oh16", "gpsimd"), ("ohcb", "ohcols", "gpsimd")]
    for nm in ("ln1w", "ln1b", "ln2w", "ln2b", "gnw", "gnb", "bigc", "bogc",
               "vnw", "vnb", "mnw", "mnb", "haspred"):
        cl.append((nm, nm, "sync"))
    cl.append(("epsc", "epsc", "sync"))
    cl.append(("prevh", "prevh", "sync"))
    for dst, src, q in cl:
        em.dma(q, sb[dst][:, :], dr[src], stream="c" + q)
    em.dma("gpsimd", sb["ones1b"][:, :], dr["ones128"][0:1, :], stream="cgpsimd")
    em.dma("gpsimd", sb["onescb"][:, :], dr["ones128"][:, 0:1],
           allow_slow_non_contiguous=True, stream="cgpsimd")
    em.dma("sync", sb["onesrf"][:, :], dr["ones128"][0:1, :], stream="csync")
    CT = (("dq_cgpsimd", em.count("dq_cgpsimd")), ("dq_csync", em.count("dq_csync")))

    ln_state = {"psum_free": None, "t1_free": [None, None]}

    def ln_group(g, src, dst, kind, wcol=None, bcol=None, extra_dep=(), xn_acc=None):
        use_toks = []
        sq_free = [None, None]
        for jc in range(NCH):
            slot = jc % 2
            xg = sb["XGB"][:, slot * GRP:(slot + 1) * GRP]
            dep = (list(use_toks[jc - 2]) if jc >= 2 else []) + list(extra_dep) + list(CT)
            tld = em.dma("gpsimd", xg, src[jc * 128:(jc + 1) * 128, g * GRP:(g + 1) * GRP],
                         deps=dep, stream=f"lx{slot}")
            xsq = sb["XSQ"][:, slot * GRP:(slot + 1) * GRP]
            em.waits("scalar", [tld, sq_free[slot]])
            tsq = em.emit("scalar", lambda E, o=xsq, i=xg: E.activation(o, i, AF.Square),
                          inc="act")
            em.waits("tensor", [tld, ln_state["psum_free"]])
            t1 = em.emit("tensor", lambda E, i=xg, jc=jc: E.matmul(
                ps[2][0:1, 0:GRP], sb["onescb"][:, 0:1], i,
                start=(jc == 0), stop=(jc == NCH - 1)), inc="pe")
            em.wait("tensor", tsq)
            t2 = em.emit("tensor", lambda E, i=xsq, jc=jc: E.matmul(
                ps[3][0:1, 0:GRP], sb["onescb"][:, 0:1], i,
                start=(jc == 0), stop=(jc == NCH - 1)), inc="pe")
            use_toks.append((t1, tsq))
            sq_free[slot] = t2
        tok_sum, tok_ssq = use_toks[-1][0], sq_free[(NCH - 1) % 2]
        m, var = sb["T1"][0:1, 0:GRP], sb["T2"][0:1, 0:GRP]
        tmp, rho = sb["MB"][0:1, 0:GRP], sb["MB"][0:1, GRP:2 * GRP]
        em.waits("vector", [tok_sum, tok_ssq] + list(ln_state["t1_free"]))
        em.emit("vector", lambda E: E.tensor_scalar_mul(m, ps[2][0:1, 0:GRP], 1.0 / D))
        em.emit("vector", lambda E: E.tensor_scalar_mul(var, ps[3][0:1, 0:GRP], 1.0 / D))
        em.emit("vector", lambda E: E.tensor_mul(tmp, m, m))
        tv = em.emit("vector", lambda E: E.tensor_sub(var, var, tmp), inc="dve")
        em.wait("scalar", tv)
        ep = sb["epsc"][0:1, 0:1]
        if kind in ("plain", "affine"):
            ts = em.emit("scalar", lambda E: E.activation(tmp, var, AF.Sqrt, bias=ep),
                         inc="act")
            em.wait("vector", ts)
            tr = em.emit("vector", lambda E: E.reciprocal(rho, tmp), inc="dve")
        else:
            ts = em.emit("scalar", lambda E: E.activation(tmp, var, AF.Sqrt, bias=ep),
                         inc="act")
            em.wait("vector", ts)
            r1 = sb["DOWN"][0:1, 0:GRP]
            em.emit("vector", lambda E: E.reciprocal(r1, tmp))
            em.emit("vector", lambda E: E.tensor_mul(tmp, var, r1))
            tv2 = em.emit("vector", lambda E: E.tensor_mul(tmp, tmp, r1), inc="dve")
            em.wait("scalar", tv2)
            ts2 = em.emit("scalar", lambda E: E.activation(tmp, tmp, AF.Sqrt, bias=ep),
                          inc="act")
            em.wait("vector", ts2)
            em.emit("vector", lambda E: E.reciprocal(tmp, tmp))
            tr = em.emit("vector", lambda E: E.tensor_mul(rho, r1, tmp), inc="dve")
        em.wait("tensor", tr)
        em.emit("tensor", lambda E: E.matmul(ps[2][:, 0:GRP], sb["onesrf"][0:1, :], m,
                                             start=True, stop=True))
        tbc = em.emit("tensor", lambda E: E.matmul(ps[3][:, 0:GRP], sb["onesrf"][0:1, :],
                                                   rho, start=True, stop=True), inc="pe")
        if xn_acc is not None:
            tbc = em.emit("tensor", lambda E: E.matmul(ps["K"][:, 0:GRP], sb["onesrf"][0:1, :],
                                                       sb["DOWN"][0:1, 0:GRP],
                                                       start=True, stop=True), inc="pe")
        em.wait("vector", tbc)
        tmb = em.emit("vector", lambda E: E.tensor_copy(sb["MB"][:, 0:GRP], ps[3][:, 0:GRP]),
                      inc="dve")
        ready = []
        last_tt1 = None
        for jc in range(NCH):
            slot = jc % 2
            xg = sb["XGB"][:, slot * GRP:(slot + 1) * GRP]
            dep = [ready[jc - 2] if jc >= 2 else None, tbc]
            tld = em.dma("gpsimd", xg, src[jc * 128:(jc + 1) * 128, g * GRP:(g + 1) * GRP],
                         deps=dep, stream=f"lx{slot}")
            t1buf = sb["T1"] if slot == 0 else sb["T2"]
            em.waits("vector", [tld, ln_state["t1_free"][slot]])
            tt1 = em.emit("vector", lambda E, o=t1buf, i=xg: E.tensor_sub(o[:, 0:GRP], i, ps[2][:, 0:GRP]),
                          inc="dve")
            last_tt1 = tt1
            if xn_acc is not None:
                dst_dram, acc_dep, dst2 = xn_acc
                em.waits("vector", [ln_state.get("xn_free")])
                txn = em.emit("vector", lambda E, i=t1buf: E.tensor_mul(
                    sb["DOWN"][:, GRP:2 * GRP], i[:, 0:GRP], ps["K"][:, 0:GRP]), inc="dve")
                txa = em.dma("gpsimd", dst_dram[jc * 128:(jc + 1) * 128, g * GRP:(g + 1) * GRP],
                             sb["DOWN"][:, GRP:2 * GRP], deps=[txn, acc_dep],
                             accum_op=ALU.add, stream="yxn")
                txa2 = em.dma("gpsimd", dst2[jc * 128:(jc + 1) * 128, g * GRP:(g + 1) * GRP],
                              sb["DOWN"][:, GRP:2 * GRP], deps=[txn], stream="yxn")
                ln_state["xn_free"] = txa2
            em.waits("gpsimd", [tt1, tmb])
            if kind == "plain":
                tt2 = em.emit("gpsimd", lambda E, o=dst(jc), i=t1buf: E.tensor_mul(
                    o, i[:, 0:GRP], sb["MB"][:, 0:GRP]), inc="pool")
                ln_state["t1_free"][slot] = tt2
                ready.append(tt2)
            else:
                t2b = sb["GS"][:, slot * GRP:(slot + 1) * GRP]
                tt2 = em.emit("gpsimd", lambda E, o=t2b, i=t1buf: E.tensor_mul(
                    o, i[:, 0:GRP], sb["MB"][:, 0:GRP]), inc="pool")
                ln_state["t1_free"][slot] = tt2
                em.waits("vector", [tt2])
                ta = em.emit("vector", lambda E, o=dst(jc), i=t2b, jc=jc: E.tensor_scalar(
                    o, i, wcol[:, jc:jc + 1], bcol[:, jc:jc + 1], ALU.mult, ALU.add),
                    inc="dve")
                ready.append(ta)
        ln_state["psum_free"] = last_tt1
        return ready

    def ffn_phase(src, dst, gate_blk, up_blk, down_blk, ln_kind="plain",
                  wcol=None, bcol=None, pre_dep=(), nq=4, acc_pre=()):
        h2_toks = []
        for g in range(NG):
            h2_toks += ln_group(g, src,
                                lambda jc, g=g: sb["BIGA"][:, dcol(jc, g * GRP):dcol(jc, (g + 1) * GRP)],
                                ln_kind, wcol, bcol, extra_dep=pre_dep)
        h2dep = tuple(h2_toks)
        acc_toks = []
        wx_free = [None, None]
        wy_free = [None, None]
        wz_free = [None, None]
        gs_free = [None, None]
        dn_free = [None, None]
        act_free = [None]
        gate_bank = [ps[0], ps[1]]
        up_bank = [ps[2], ps[3]]
        down_bank = [ps["S"][:, 0:512], ps["S"][:, 512:1024]]
        gate_bank_free = [None, None]
        up_bank_free = [None, None]
        down_bank_free = [ln_state["psum_free"], None]
        for p in range(nq):
            act_ready = []
            for i in range(NCH):
                ic = p * NCH + i
                ws = i % 2
                wx = sb["WX"][:, ws * 2048:(ws + 1) * 2048]
                wy = sb["WY"][:, ws * 2048:(ws + 1) * 2048]
                twx = em.dma("gpsimd", wx, gate_blk(ic), deps=[wx_free[ws]] + list(CT), stream=f"wg{ws}")
                twy = em.dma("gpsimd", wy, up_blk(ic), deps=[wy_free[ws]], stream=f"wu{ws}")
                for g in range(NG):
                    gb = g % 2
                    em.waits("tensor", [twx, gate_bank_free[gb]] + list(h2dep))
                    for j in range(NCH):
                        tg = em.emit("tensor", lambda E, j=j, wx=wx, g=g, gb=gb: E.matmul(
                            gate_bank[gb][:, 0:GRP], wx[:, j * 128:(j + 1) * 128],
                            sb["BIGA"][:, dcol(j, g * GRP):dcol(j, (g + 1) * GRP)],
                            start=(j == 0), stop=(j == NCH - 1)),
                            inc="pe" if j == NCH - 1 else None)
                    em.waits("tensor", [twy, up_bank_free[gb]])
                    for j in range(NCH):
                        tu = em.emit("tensor", lambda E, j=j, wy=wy, g=g, gb=gb: E.matmul(
                            up_bank[gb][:, 0:GRP], wy[:, j * 128:(j + 1) * 128],
                            sb["BIGA"][:, dcol(j, g * GRP):dcol(j, (g + 1) * GRP)],
                            start=(j == 0), stop=(j == NCH - 1)),
                            inc="pe" if j == NCH - 1 else None)
                    if g == NG - 1:
                        wx_free[ws] = tg
                        wy_free[ws] = tu
                    gs = sb["GS"][:, gb * GRP:(gb + 1) * GRP]
                    em.waits("scalar", [tg, gs_free[gb]])
                    tsl = em.emit("scalar", lambda E, gs=gs, gb=gb: E.activation(
                        gs, gate_bank[gb][:, 0:GRP], AF.Silu), inc="act")
                    gate_bank_free[gb] = tsl
                    em.waits("vector", [tsl, tu, act_free[0]])
                    tact = em.emit("vector", lambda E, gs=gs, i=i, g=g, gb=gb: E.tensor_mul(
                        sb["BIGB"][:, dcol(i, g * GRP):dcol(i, (g + 1) * GRP)],
                        gs, up_bank[gb][:, 0:GRP]), inc="dve")
                    gs_free[gb] = tact
                    up_bank_free[gb] = tact
                    act_ready.append(tact)
            last_dn = None
            for hc in range(NCH):
                ws = hc % 2
                wz = sb["WZ"][:, ws * 2048:(ws + 1) * 2048]
                twz = em.dma("gpsimd", wz, down_blk(hc, p), deps=[wz_free[ws]], stream=f"wd{ws}")
                for g in range(NG):
                    db = (hc * NG + g) % 2
                    em.waits("tensor", [twz, down_bank_free[db]] + act_ready)
                    for i in range(NCH):
                        td = em.emit("tensor", lambda E, i=i, wz=wz, g=g, db=db: E.matmul(
                            down_bank[db][:, 0:GRP], wz[:, i * 128:(i + 1) * 128],
                            sb["BIGB"][:, dcol(i, g * GRP):dcol(i, (g + 1) * GRP)],
                            start=(i == 0), stop=(i == NCH - 1)),
                            inc="pe" if i == NCH - 1 else None)
                    last_dn = td
                    if g == NG - 1:
                        wz_free[ws] = td
                    dslot = sb["DOWN"][:, db * GRP:(db + 1) * GRP]
                    em.waits("scalar", [td, dn_free[db]])
                    tev = em.emit("scalar", lambda E, dslot=dslot, db=db: E.copy(
                        dslot, down_bank[db][:, 0:GRP]), inc="act")
                    down_bank_free[db] = tev
                    accdep = [tev]
                    if hc == 0 and g == 0:
                        accdep.append((f"dq_acc{(p - 1) % 2}", em.count(f"dq_acc{(p - 1) % 2}")))
                        accdep += list(acc_pre)
                    tac = em.dma("gpsimd", dst[hc * 128:(hc + 1) * 128, g * GRP:(g + 1) * GRP],
                                 dslot, deps=accdep, accum_op=ALU.add, stream=f"acc{p % 2}")
                    dn_free[db] = tac
                    acc_toks.append(tac)
            act_free[0] = last_dn
        return [("dq_acc0", em.count("dq_acc0")), ("dq_acc1", em.count("dq_acc1"))]

    def blk3(ap3):
        return ap3.transpose([1, 0, 2])

    phase_end = []

    if "p1" in phases:
        phase_end = attn_a_phase(nc, em, dr, sb, ps, TPC, NG, NT, CS, CT, dcol,
                                 ln_group, ln_state, blk3)
    else:
        t = em.dma("sync", dr["xb1"], dr["x0"], stream="cp")
        phase_end = [t]

    if "dbg_x1" in ("dbg_" + n for n in debug):
        phase_end = list(phase_end)
        phase_end.append(em.dma("sync", dr["dbg_x1"], dr["xb1"], deps=phase_end, stream="cp"))

    if "p2" in phases:
        acc = ffn_phase(
            dr["xb1"], dr["xb1"],
            gate_blk=lambda ic: blk3(dr["wg1"][ic]),
            up_blk=lambda ic: blk3(dr["wg1"][64 + ic]),
            down_blk=lambda hc, p: blk3(dr["wd1"][hc, p * NCH:(p + 1) * NCH]),
            ln_kind="plain", pre_dep=tuple(phase_end), acc_pre=tuple(phase_end))
        phase_end = acc
    if "dbg_x2" in ("dbg_" + n for n in debug):
        phase_end = list(phase_end)
        phase_end.append(em.dma("sync", dr["dbg_x2"], dr["xb1"], deps=phase_end, stream="cp"))

    if "p3" in phases:
        phase_end = attn_b_phase(nc, em, dr, sb, ps, TPC, NG, NT, CS, CT, dcol,
                                 ln_group, ln_state, blk3, ffn_phase, phase_end)
    else:
        t = em.dma("sync", dr["y"], dr["xb1"], deps=list(phase_end), stream="cp")
        phase_end = [t]
    if "dbg_x3" in ("dbg_" + n for n in debug):
        phase_end = list(phase_end)
        phase_end.append(em.dma("sync", dr["dbg_x3"], dr["y"], deps=phase_end, stream="cp"))

    if "p4" in phases:
        acc = ffn_phase(
            dr["y"], dr["y"],
            gate_blk=lambda ic: blk3(dr["wg1"][ic]),
            up_blk=lambda ic: blk3(dr["wg1"][64 + ic]),
            down_blk=lambda hc, p: blk3(dr["wd1"][hc, p * NCH:(p + 1) * NCH]),
            ln_kind="plain", pre_dep=tuple(phase_end), acc_pre=tuple(phase_end))
        phase_end = acc
    else:
        if "p3" not in phases and "p2" not in phases and "p1" not in phases:
            pass

    em.waits("gpsimd", list(phase_end))
    em.waits("sync", list(phase_end))
    for snm in list(em._sems):
        if snm.startswith("dq_"):
            em.wait("gpsimd", (snm, em.count(snm)))
            em.wait("sync", (snm, em.count(snm)))


def attn_a_phase(nc, em, dr, sb, ps, TPC, NG, NT, CS, CT, dcol, ln_group,
                 ln_state, blk3):
    GPT = GRP // TIL
    BIGA, BIGB = sb["BIGA"], sb["BIGB"]
    AF_ = lambda a, b: BIGA[:, a:b]
    t_cp = em.dma("sync", dr["xb1"], dr["x0"], stream="cp")

    h1_toks = []
    for g in range(NG):
        h1_toks += ln_group(g, dr["x0"],
                            lambda jc, g=g: BIGA[:, dcol(jc, g * GRP):dcol(jc, (g + 1) * GRP)],
                            "plain")
    h1dep = tuple(h1_toks)

    em.dma("gpsimd", sb["WZ"][:, 0:2048], blk3(dr["wfq"]), stream="wfm")
    em.dma("gpsimd", sb["WZ"][:, 2048:4096], blk3(dr["wfk"]), stream="wfm")
    tfm = ("dq_wfm", em.count("dq_wfm"))
    WFQ, WFK = sb["WZ"][:, 0:2048], sb["WZ"][:, 2048:4096]

    em.waits("vector", list(ln_state["t1_free"]))
    em.emit("vector", lambda E: E.memset(sb["MB"][:, 0:1024], 0.0))
    tz0 = em.emit("vector", lambda E: E.memset(sb["KS"][:, 0:16], 0.0), inc="dve")

    GPT_ = GRP // TIL
    em.emit("vector", lambda E: E.memset(sb["FKCZ"][0:64, :], 0.0))
    wq_free = [None, None]; wk_free = [None, None]
    qsl_free = None; p0_free = None; p1_free = None
    fkc_free = None; pb_free = None; p2_free = None
    amt_free = [None, None]; fkt_free = [None, None]
    tq = tk = None
    for ic in range(NCH):
        ws = ic % 2
        wq = sb["WX"][:, ws * 2048:(ws + 1) * 2048]
        wk = sb["WY"][:, ws * 2048:(ws + 1) * 2048]
        twq = em.dma("gpsimd", wq, blk3(dr["waq"][ic]), deps=[wq_free[ws]] + list(CT), stream=f"wq{ws}")
        twk = em.dma("gpsimd", wk, blk3(dr["wak"][ic]), deps=[wk_free[ws]], stream=f"wk{ws}")
        for g in range(NG):
            em.waits("tensor", [twq, p0_free] + list(h1dep))
            for j in range(NCH):
                tq = em.emit("tensor", lambda E, j=j, wq=wq, g=g: E.matmul(
                    ps[0][:, 0:GRP], wq[:, j * 128:(j + 1) * 128],
                    BIGA[:, dcol(j, g * GRP):dcol(j, (g + 1) * GRP)],
                    start=(j == 0), stop=(j == NCH - 1)),
                    inc="pe" if j == NCH - 1 else None)
            em.waits("vector", [tq, qsl_free])
            tev = em.emit("vector", lambda E: E.tensor_copy(sb["QSL"][:, 0:GRP], ps[0][:, 0:GRP]),
                          inc="dve")
            p0_free = tev
            em.waits("tensor", [tev, tfm, p1_free])
            tzq = em.emit("tensor", lambda E, ic=ic: E.matmul(
                ps[1][:, 0:GRP], WFQ[:, ic * 128:(ic + 1) * 128], sb["QSL"][:, 0:GRP],
                start=True, stop=True), inc="pe")
            em.wait("scalar", tzq)
            tfqx = em.emit("scalar", lambda E, ic=ic, g=g: E.activation(
                BIGB[:, dcol(ic, g * GRP):dcol(ic, (g + 1) * GRP)], ps[1][:, 0:GRP],
                AF.Exp), inc="act")
            p1_free = tfqx
            em.waits("tensor", [twk, p0_free])
            for j in range(NCH):
                tk = em.emit("tensor", lambda E, j=j, wk=wk, g=g: E.matmul(
                    ps[0][:, 0:GRP], wk[:, j * 128:(j + 1) * 128],
                    BIGA[:, dcol(j, g * GRP):dcol(j, (g + 1) * GRP)],
                    start=(j == 0), stop=(j == NCH - 1)),
                    inc="pe" if j == NCH - 1 else None)
            em.waits("vector", [tk, tzq])
            tev2 = em.emit("vector", lambda E: E.tensor_copy(sb["QSL"][:, 0:GRP], ps[0][:, 0:GRP]),
                           inc="dve")
            p0_free = tev2
            em.waits("tensor", [tev2, tfqx, fkc_free])
            tzk = em.emit("tensor", lambda E, ic=ic: E.matmul(
                ps[1][:, 0:GRP], WFK[:, ic * 128:(ic + 1) * 128], sb["QSL"][:, 0:GRP],
                start=True, stop=True), inc="pe")
            qsl_free = tzk
            em.wait("scalar", tzk)
            em.emit("scalar", lambda E: E.activation(
                sb["FKC"][:, 0:GRP], ps[1][:, 0:GRP], AF.Exp))
            tfkx = em.emit("scalar", lambda E: E.activation(
                sb["FKCZ"][64:128, 0:GRP], ps[1][64:128, 0:GRP], AF.Exp), inc="act")
            p1_free = tfkx
            last_cons = None
            for tt in range(GPT_):
                t = g * GPT_ + tt
                sl = t % 2
                fkt = sb["GS"][:, sl * 128:(sl + 1) * 128]
                amt = sb["GS"][:, 512 + sl * 256:512 + (sl + 1) * 256]
                em.waits("tensor", [tfkx, pb_free] + list(CT))
                ttr = em.emit("tensor", lambda E, tt=tt: E.transpose(
                    ps["B"][:, 0:128], sb["FKC"][:, tt * 128:(tt + 1) * 128], sb["idenb"][:, :]),
                    inc="pe")
                em.waits("vector", [ttr, fkt_free[sl]])
                tev3 = em.emit("vector", lambda E, fkt=fkt: E.tensor_copy(fkt, ps["B"][:, 0:128]),
                               inc="dve")
                pb_free = tev3
                tzkr = em.emit("vector", lambda E, fkt=fkt, t=t, ic=ic: E.reduce_sum(
                    sb["W32"][:, t * 32 + 2 * ic:t * 32 + 2 * ic + 2],
                    fkt.rearrange("p (h f) -> p h f", h=2), axis=AX.X), inc="dve")
                tsp1 = em.dma("sync", dr["fkt_sp"][t, ic], fkt, deps=[tev3, tzkr], stream=f"fkt{sl}")
                fkt_free[sl] = tsp1
                em.waits("tensor", [p2_free, tev3])
                em.emit("tensor", lambda E, tt=tt, ic=ic, g=g: E.matmul(
                    ps[2][:, 0:128],
                    sb["FKC"][0:64, tt * 128:(tt + 1) * 128],
                    BIGB[0:64, dcol(ic, g * GRP + tt * 128):dcol(ic, g * GRP + (tt + 1) * 128)],
                    start=True, stop=True))
                ta = em.emit("tensor", lambda E, tt=tt, ic=ic, g=g: E.matmul(
                    ps[2][:, 128:256],
                    sb["FKCZ"][:, tt * 128:(tt + 1) * 128],
                    BIGB[:, dcol(ic, g * GRP + tt * 128):dcol(ic, g * GRP + (tt + 1) * 128)],
                    start=True, stop=True), inc="pe")
                last_cons = ta
                em.waits("vector", [ta, amt_free[sl]])
                tmask = em.emit("vector", lambda E, amt=amt: E.tensor_mul(
                    amt.rearrange("p (h t) -> p h t", h=2),
                    ps[2][:, 0:256].rearrange("p (h t) -> p h t", h=2),
                    sb["trilT"][:, :].unsqueeze(1).broadcast_to([128, 2, 128])), inc="dve")
                p2_free = tmask
                tsp2 = em.dma("sync", dr["amt_sp"][t, ic], amt, deps=[tmask], stream=f"amt{sl}")
                amt_free[sl] = tsp2
            fkc_free = last_cons
        wq_free[ws] = tq
        wk_free[ws] = tk
    trc = em.emit("vector", lambda E: E.reciprocal(sb["W32"][:, 0:NT * 32], sb["W32"][:, 0:NT * 32]),
                  inc="dve")
    em.wait("vector", trc)
    twb = em.emit("vector", lambda E: E.tensor_copy(sb["W32B"][:, 0:NT * 32], sb["W32"][:, 0:NT * 32]),
                  inc="dve")

    spills_done = [("dq_fkt0", em.count("dq_fkt0")), ("dq_fkt1", em.count("dq_fkt1")),
                   ("dq_amt0", em.count("dq_amt0")), ("dq_amt1", em.count("dq_amt1"))]
    em.waits("vector", spills_done)
    vsl_free = None
    vpt_free = [None, None]
    amtr_free = [None, None]
    fktr_free = [None, None]
    yin_free = [None, None]
    pv_free = [None, None]
    py_free = None
    pdelta_free = None
    sacc = tz0
    tkc = None
    FKTR = [sb["QSL"][:, 0:512], sb["T2"][:, :].bitcast(BF16)[:, 0:512]]
    YIN = [sb["T1"][:, :].bitcast(BF16)[:, 0:512], sb["T2"][:, :].bitcast(BF16)[:, 512:1024]]
    for isl in range(4):
        vdep = [vsl_free, wq_free[0], wq_free[1], wk_free[0], wk_free[1]]
        wavv = dr["wav"][:, isl * GRP:(isl + 1) * GRP]
        em.dma("gpsimd", sb["WX"][:, :],
               wavv[0:1024, :].rearrange("(c p) i -> p c i", p=128), deps=vdep, stream="wv")
        em.dma("gpsimd", sb["WY"][:, :],
               wavv[1024:2048, :].rearrange("(c p) i -> p c i", p=128), stream="wv")
        tvw = ("dq_wv", em.count("dq_wv"))
        for t in range(NT):
            sl = t % 2
            pv = ps[0] if sl == 0 else ps[1]
            em.waits("tensor", [tvw, pv_free[sl]] + list(h1dep))
            tvm = None
            for j in range(NCH):
                wsl = sb["WX"][:, j * GRP:(j + 1) * GRP] if j < 8 else sb["WY"][:, (j - 8) * GRP:(j - 7) * GRP]
                tvm = em.emit("tensor", lambda E, j=j, t=t, wsl=wsl, pv=pv: E.matmul(
                    pv[:, 0:GRP], BIGA[:, dcol(j, t * 128):dcol(j, (t + 1) * 128)], wsl,
                    start=(j == 0), stop=(j == NCH - 1)),
                    inc="pe" if j == NCH - 1 else None)
            vsl_free = tvm
            vpt = sb["GS"][:, sl * GRP:(sl + 1) * GRP]
            em.waits("vector", [tvm, vpt_free[sl], twb,
                                (f"dq_vp{sl}", em.count(f"dq_vp{sl}"))])
            tvp = em.emit("vector", lambda E, vpt=vpt, pv=pv, t=t, isl=isl: E.tensor_mul(
                vpt.rearrange("p (h d) -> p h d", h=8),
                pv[:, 0:GRP].rearrange("p (h d) -> p h d", h=8),
                sb["W32"][:, t * 32 + 8 * isl:t * 32 + 8 * isl + 8].unsqueeze(2).broadcast_to([128, 8, 64])),
                inc="dve")
            pv_free[sl] = tvp
            tsp3 = em.dma("sync", dr["vp_sp"][t, 4 * isl:4 * isl + 4].transpose([1, 0, 2]),
                          vpt.rearrange("p (b c) -> p b c", b=4), deps=[tvp], stream=f"vp{sl}")
            amtr = sb["XGB"][:, 0:1024] if sl == 0 else sb["XSQ"][:, 0:1024]
            tld = em.dma("sync", amtr.rearrange("p (b c) -> p b c", b=4),
                         dr["amt_sp"][t, 4 * isl:4 * isl + 4].transpose([1, 0, 2]),
                         deps=[amtr_free[sl]] + spills_done, stream=f"ar{sl}")
            tld2 = em.dma("sync", FKTR[sl].rearrange("p (b c) -> p b c", b=4),
                          dr["fkt_sp"][t, 4 * isl:4 * isl + 4].transpose([1, 0, 2]),
                          deps=[fktr_free[sl]] + spills_done, stream=f"fr{sl}")
            em.waits("tensor", [tld, tld2, tvp, py_free, pdelta_free, twb])
            for cc in range(4):
                for par in range(2):
                    h8 = 2 * cc + par
                    em.emit("tensor", lambda E, cc=cc, par=par, h8=h8, vpt=vpt, amtr=amtr: E.matmul(
                        ps[3][par * 64:(par + 1) * 64, cc * 128:(cc + 1) * 128],
                        vpt[:, h8 * 64:(h8 + 1) * 64],
                        amtr[:, h8 * 128:(h8 + 1) * 128],
                        start=True, stop=True))
                    em.emit("tensor", lambda E, h8=h8, t=t, isl=isl, amtr=amtr: E.matmul(
                        ps["K"][:, t * 32 + 8 * isl + h8:t * 32 + 8 * isl + h8 + 1],
                        amtr[:, h8 * 128:(h8 + 1) * 128],
                        sb["W32B"][:, t * 32 + 8 * isl + h8:t * 32 + 8 * isl + h8 + 1],
                        start=True, stop=True))
                    em.emit("tensor", lambda E, cc=cc, par=par, h8=h8, vpt=vpt, sl=sl: E.matmul(
                        ps["S"][par * 64:(par + 1) * 64, cc * 64:(cc + 1) * 64],
                        FKTR[sl][:, cc * 128 + par * 64:cc * 128 + par * 64 + 64],
                        vpt[:, h8 * 64:(h8 + 1) * 64],
                        start=True, stop=True))
                    tkc = em.emit("tensor", lambda E, cc=cc, par=par, h8=h8, t=t, isl=isl, sl=sl: E.matmul(
                        ps["S"][par * 64:(par + 1) * 64, 256 + cc:257 + cc],
                        FKTR[sl][:, cc * 128 + par * 64:cc * 128 + par * 64 + 64],
                        sb["W32B"][:, t * 32 + 8 * isl + h8:t * 32 + 8 * isl + h8 + 1],
                        start=True, stop=True),
                        inc="pe" if (cc == 3 and par == 1) else None)
            amtr_free[sl] = tkc
            fktr_free[sl] = tkc
            vpt_free[sl] = tkc
            em.waits("vector", [tkc, sacc])
            em.emit("vector", lambda E, isl=isl: E.tensor_add(
                sb["MB"][:, isl * 256:(isl + 1) * 256], sb["MB"][:, isl * 256:(isl + 1) * 256],
                ps["S"][:, 0:256]))
            sacc = em.emit("vector", lambda E, isl=isl: E.tensor_add(
                sb["KS"][:, 4 * isl:4 * isl + 4], sb["KS"][:, 4 * isl:4 * isl + 4],
                ps["S"][:, 256:260]), inc="dve")
            pdelta_free = sacc
            yin = YIN[sl]
            em.waits("vector", [yin_free[sl]])
            tyev = em.emit("vector", lambda E, yin=yin: E.tensor_copy(yin, ps[3][:, 0:512]),
                           inc="dve")
            py_free = tyev
            tsp4 = em.dma("sync", dr["yin_sp"][t, 4 * isl:4 * isl + 4].transpose([1, 0, 2]),
                          yin.rearrange("p (b c) -> p b c", b=4),
                          deps=[tyev], stream=f"yi{sl}")
            yin_free[sl] = tsp4
    em.waits("vector", [tkc])
    tde = em.emit("vector", lambda E: E.tensor_copy(sb["DEN"][:, 0:NT * 32], ps["K"][:, 0:NT * 32]),
                  inc="dve")
    em.waits("gpsimd", [sacc, tde])
    em.dma("gpsimd", dr["cc1i"][:, 0:1024], sb["MB"][:, 0:1024], stream="cc")
    em.dma("gpsimd", dr["cc1i"][:, 1024:1040], sb["KS"][:, 0:16], stream="cc")
    em.wait("gpsimd", ("dq_cc", em.count("dq_cc")))
    em.emit("gpsimd", lambda E: E.collective_compute(
        "AllGather", ALU.bypass, replica_groups=[list(range(NCORES))],
        ins=[dr["cc1i"]], outs=[dr["cc1o"]]), inc="cc1")
    tcc = ("cc1", 1)

    if P1STOP == "a2":
        return [("dq_cp", em.count("dq_cp")), ("pe", em.count("pe")), ("dve", em.count("dve")),
                ("act", em.count("act")), ("cc1", 1)]
    em.waits("vector", [tcc])
    em.emit("vector", lambda E: E.memset(sb["SRUN"][:, :], 0.0))
    tms = em.emit("vector", lambda E: E.memset(sb["KS"][:, 0:16], 0.0), inc="dve")
    gat_free = tms
    ta2 = tms
    for r in range(NCORES):
        tg1 = em.dma("sync", sb["DOWN"][:, 0:1024], dr["cc1o"][r * 128:(r + 1) * 128, 0:1024],
                     deps=[gat_free, tcc, ("dq_yi0", em.count("dq_yi0")),
                           ("dq_yi1", em.count("dq_yi1"))], stream="ccg")
        tg2 = em.dma("sync", sb["MB"][:, 0:16], dr["cc1o"][r * 128:(r + 1) * 128, 1024:1040],
                     deps=[("dq_cc", em.count("dq_cc"))], stream="ccg")
        em.waits("vector", [tg1, tg2])
        em.emit("vector", lambda E, r=r: E.scalar_tensor_tensor(
            sb["SRUN"][:, :], sb["DOWN"][:, 0:1024], sb["inc8"][:, r:r + 1], sb["SRUN"][:, :],
            ALU.mult, ALU.add))
        ta2 = em.emit("vector", lambda E, r=r: E.scalar_tensor_tensor(
            sb["KS"][:, 0:16], sb["MB"][:, 0:16], sb["inc8"][:, r:r + 1], sb["KS"][:, 0:16],
            ALU.mult, ALU.add), inc="dve")
        gat_free = ta2
    em.emit("vector", lambda E: E.memset(sb["SRUNZ"][0:64, :], 0.0))
    em.emit("vector", lambda E: E.memset(sb["KSZ"][0:64, :], 0.0))
    em.emit("vector", lambda E: E.tensor_copy(sb["SRUNB"][:, :], sb["SRUN"][:, :]))
    em.emit("vector", lambda E: E.tensor_copy(sb["SRUNZ"][64:128, :], sb["SRUN"][64:128, :]))
    tsb_ = em.emit("vector", lambda E: E.tensor_copy(sb["KSB"][:, 0:16], sb["KS"][:, 0:16]),
                   inc="dve")
    tksb = em.emit("vector", lambda E: E.tensor_copy(sb["KSZ"][64:128, 0:16], sb["KS"][64:128, 0:16]),
                   inc="dve")

    if P1STOP == "comb":
        return [("dq_cp", em.count("dq_cp")), ("dve", em.count("dve"))]
    FKT2 = [AF_(0, 2048), AF_(2048, 4096)]
    VPT2 = [AF_(4096, 6144), AF_(6144, 8192)]
    YIN2 = [AF_(8192, 10240), AF_(10240, 12288)]
    G1 = AF_(12288, 14336)
    YGOF = 14336
    h1_dead = vsl_free
    f2_free = [h1_dead, h1_dead]
    v2_free = [h1_dead, h1_dead]
    y2_free = [h1_dead, h1_dead]
    yg_free = None
    pk_free = None
    ptr_free = None
    srun_upd = tsb_
    ks_upd = tksb
    wo_free = [None, None]
    dn2_free = [None, None]
    pw_free = [None, None]
    tscat = None
    for g in range(NG):
        for tt in range(GPT):
            t = g * GPT + tt
            sl = t % 2
            tldf = em.dma("sync", FKT2[sl].rearrange("p (b c) -> p b c", b=16),
                          dr["fkt_sp"][t].transpose([1, 0, 2]),
                          deps=[f2_free[sl], ("dq_fr0", em.count("dq_fr0")),
                                ("dq_fr1", em.count("dq_fr1"))], stream=f"f2{sl}")
            tldv = em.dma("sync", VPT2[sl].rearrange("p (b c) -> p b c", b=16),
                          dr["vp_sp"][t].transpose([1, 0, 2]),
                          deps=[v2_free[sl], ("dq_vp0", em.count("dq_vp0")),
                                ("dq_vp1", em.count("dq_vp1"))], stream=f"v2{sl}")
            tldy = em.dma("sync", YIN2[sl].rearrange("p (b c) -> p b c", b=16),
                          dr["yin_sp"][t].transpose([1, 0, 2]),
                          deps=[y2_free[sl], ("dq_yi0", em.count("dq_yi0")),
                                ("dq_yi1", em.count("dq_yi1"))], stream=f"y2{sl}")
            em.waits("tensor", [srun_upd, yg_free, tldy])
            ty2 = None
            for c in range(NCH):
                bank = ps[c // 4]
                em.emit("tensor", lambda E, c=c, t=t, bank=bank: E.matmul(
                    bank[0:64, (c % 4) * 128:(c % 4 + 1) * 128],
                    sb["SRUNB"][0:64, c * 64:(c + 1) * 64],
                    BIGB[0:64, dcol(c, t * 128):dcol(c, (t + 1) * 128)],
                    start=True, stop=True))
                ty2 = em.emit("tensor", lambda E, c=c, t=t, bank=bank: E.matmul(
                    bank[64:128, (c % 4) * 128:(c % 4 + 1) * 128],
                    sb["SRUNZ"][:, c * 64:(c + 1) * 64],
                    BIGB[:, dcol(c, t * 128):dcol(c, (t + 1) * 128)],
                    start=True, stop=True),
                    inc="pe" if c == NCH - 1 else None)
            em.waits("tensor", [ks_upd, pk_free])
            tdi = None
            for c in range(NCH):
                em.emit("tensor", lambda E, c=c, t=t: E.matmul(
                    ps["K"][:, 2 * c:2 * c + 1],
                    BIGB[0:64, dcol(c, t * 128):dcol(c, (t + 1) * 128)],
                    sb["KSB"][0:64, c:c + 1],
                    start=True, stop=True))
                tdi = em.emit("tensor", lambda E, c=c, t=t: E.matmul(
                    ps["K"][:, 2 * c + 1:2 * c + 2],
                    BIGB[:, dcol(c, t * 128):dcol(c, (t + 1) * 128)],
                    sb["KSZ"][:, c:c + 1],
                    start=True, stop=True), inc="pe" if c == NCH - 1 else None)
            em.waits("vector", [tdi, ("dq_yi0", em.count("dq_yi0")), ("dq_yi1", em.count("dq_yi1"))])
            em.emit("vector", lambda E, t=t: E.tensor_add(sb["T1"][:, 0:32], ps["K"][:, 0:32],
                                                          sb["DEN"][:, t * 32:(t + 1) * 32]))
            em.emit("vector", lambda E: E.tensor_scalar_add(sb["T1"][:, 0:32], sb["T1"][:, 0:32],
                                                            sb["epsc"][:, 1:2]))
            trr = em.emit("vector", lambda E: E.reciprocal(sb["T1"][:, 32:64], sb["T1"][:, 0:32]),
                          inc="dve")
            pk_free = trr
            em.waits("tensor", [trr, ptr_free])
            em.emit("tensor", lambda E: E.transpose(
                ps["K"][0:16, 128:256], sb["T1"][:, 32:64:2], sb["iden"][:, :]))
            ttr2 = em.emit("tensor", lambda E: E.transpose(
                ps["K"][0:16, 384:512], sb["T1"][:, 33:64:2], sb["iden"][:, :]), inc="pe")
            em.wait("vector", ttr2)
            em.emit("vector", lambda E: E.tensor_copy(sb["ABFM"][0:16, 0:128],
                                                      ps["K"][0:16, 128:256]))
            trab = em.emit("vector", lambda E: E.tensor_copy(sb["ABFM"][0:16, 128:256],
                                                             ps["K"][0:16, 384:512]), inc="dve")
            ptr_free = trab
            em.waits("vector", [ty2, tldy, tscat])
            tg1y = None
            for b in range(4):
                tg1y = em.emit("vector", lambda E, b=b, sl=sl: E.tensor_add(
                    G1[:, b * 512:(b + 1) * 512], ps[b][:, 0:512],
                    YIN2[sl][:, b * 512:(b + 1) * 512]), inc="dve")
            em.waits("tensor", [trab, tg1y])
            trb = None
            for c in range(NCH):
                bank = ps[c // 4]
                em.emit("tensor", lambda E, c=c, bank=bank: E.matmul(
                    bank[0:64, (c % 4) * 128:(c % 4 + 1) * 128],
                    sb["oh16b"][0:16, c * 128:c * 128 + 64], sb["ABFM"][0:16, 0:128],
                    start=True, stop=True))
                trb = em.emit("tensor", lambda E, c=c, bank=bank: E.matmul(
                    bank[64:128, (c % 4) * 128:(c % 4 + 1) * 128],
                    sb["oh16b"][0:16, c * 128:c * 128 + 64], sb["ABFM"][0:16, 128:256],
                    start=True, stop=True), inc="pe" if c == NCH - 1 else None)
            em.waits("vector", [trb])
            ty = None
            for b in range(4):
                ty = em.emit("vector", lambda E, b=b: E.tensor_mul(
                    G1[:, b * 512:(b + 1) * 512],
                    G1[:, b * 512:(b + 1) * 512], ps[b][:, 0:512]), inc="dve")
            yg_free = ty
            y2_free[sl] = ty
            em.waits("gpsimd", [ty])
            for c in range(NCH):
                tscat = em.emit("gpsimd", lambda E, c=c, tt=tt: E.tensor_copy(
                    BIGA[:, YGOF + c * GRP + tt * 128:YGOF + c * GRP + tt * 128 + 128],
                    G1[:, c * 128:(c + 1) * 128]), inc="pool" if c == NCH - 1 else None)
            em.waits("tensor", [tg1y, tdi])
            tdk = None
            for c in range(NCH):
                for par in range(2):
                    em.emit("tensor", lambda E, c=c, par=par, sl=sl: E.matmul(
                        ps["S"][par * 64:(par + 1) * 64, c * 64:(c + 1) * 64],
                        FKT2[sl][:, c * 128 + par * 64:c * 128 + par * 64 + 64],
                        VPT2[sl][:, (2 * c + par) * 64:(2 * c + par + 1) * 64],
                        start=True, stop=True))
                    tdk = em.emit("tensor", lambda E, c=c, par=par, sl=sl, t=t: E.matmul(
                        ps["K"][par * 64:(par + 1) * 64, 256 + c:257 + c],
                        FKT2[sl][:, c * 128 + par * 64:c * 128 + par * 64 + 64],
                        sb["W32B"][:, t * 32 + 2 * c + par:t * 32 + 2 * c + par + 1],
                        start=True, stop=True),
                        inc="pe" if (c == NCH - 1 and par == 1) else None)
            f2_free[sl] = tdk
            v2_free[sl] = tdk
            pk_free = tdk
            em.waits("vector", [tdk])
            em.emit("vector", lambda E: E.tensor_add(sb["SRUN"][:, :], sb["SRUN"][:, :],
                                                     ps["S"][:, 0:1024]))
            em.emit("vector", lambda E: E.tensor_add(sb["KS"][:, 0:16], sb["KS"][:, 0:16],
                                                     ps["K"][:, 256:272]))
            em.emit("vector", lambda E: E.tensor_copy(sb["SRUNB"][:, :], sb["SRUN"][:, :]))
            em.emit("vector", lambda E: E.tensor_copy(sb["SRUNZ"][64:128, :], sb["SRUN"][64:128, :]))
            srun_upd = em.emit("vector", lambda E: E.tensor_copy(sb["KSB"][:, 0:16], sb["KS"][:, 0:16]),
                               inc="dve")
            ks_upd = em.emit("vector", lambda E: E.tensor_copy(sb["KSZ"][64:128, 0:16], sb["KS"][64:128, 0:16]),
                             inc="dve")
        for oc in range(NCH):
            ws2 = oc % 2
            wo = sb["WX"][:, ws2 * 2048:(ws2 + 1) * 2048]
            two = em.dma("gpsimd", wo, blk3(dr["wao"][oc]),
                         deps=[wo_free[ws2] if wo_free[ws2] is not None else h1_dead,
                               ("dq_wv", em.count("dq_wv"))], stream=f"wo{ws2}")
            pb2 = ps[0] if ws2 == 0 else ps[1]
            em.waits("tensor", [two, pw_free[ws2], tscat])
            tom = None
            for j in range(NCH):
                tom = em.emit("tensor", lambda E, j=j, wo=wo, pb2=pb2: E.matmul(
                    pb2[:, 0:GRP], wo[:, j * 128:(j + 1) * 128],
                    BIGA[:, YGOF + j * GRP:YGOF + (j + 1) * GRP],
                    start=(j == 0), stop=(j == NCH - 1)),
                    inc="pe" if j == NCH - 1 else None)
            wo_free[ws2] = tom
            yg_free = tom
            dslot = sb["DOWN"][:, ws2 * GRP:(ws2 + 1) * GRP]
            em.waits("scalar", [tom, dn2_free[ws2], ("dq_ccg", em.count("dq_ccg"))])
            tev4 = em.emit("scalar", lambda E, dslot=dslot, pb2=pb2: E.copy(dslot, pb2[:, 0:GRP]),
                           inc="act")
            pw_free[ws2] = tev4
            tac = em.dma("gpsimd", dr["xb1"][oc * 128:(oc + 1) * 128, g * GRP:(g + 1) * GRP],
                         dslot, deps=[tev4, t_cp, ("dq_xacc0", em.count("dq_xacc0")) if (g > 0 and oc == 0) else None],
                         accum_op=ALU.add, stream="xacc0")
            dn2_free[ws2] = tac
    return [("dq_xacc0", em.count("dq_xacc0"))]

def attn_b_phase(nc, em, dr, sb, ps, TPC, NG, NT, CS, CT, dcol, ln_group,
                 ln_state, blk3, ffn_phase, pre):
    BIGA, BIGB = sb["BIGA"], sb["BIGB"]
    XF = sb["XCG"][:, :].bitcast(F32)
    NTS = TPC // 128
    t_cp2 = em.dma("sync", dr["y"], dr["xb1"], deps=list(pre), stream="cp")

    hb_toks = []
    for g in range(NG):
        hb_toks += ln_group(g, dr["xb1"],
                            lambda jc, g=g: BIGA[:, dcol(jc, g * GRP):dcol(jc, (g + 1) * GRP)],
                            "double", sb["ln1w"], sb["ln1b"], extra_dep=list(pre),
                            xn_acc=(dr["y"], t_cp2, dr["xb1"]))
    hbdep = tuple(hb_toks)
    txn_done = ("dq_yxn", em.count("dq_yxn"))

    if P3STOP == "ln":
        return [("dq_yxn", em.count("dq_yxn")), ("dq_cp", em.count("dq_cp")),
                ("pool", em.count("pool")), ("dve", em.count("dve")), ("act", em.count("act"))]
    em.waits("vector", list(hbdep))
    th = em.emit("vector", lambda E: E.tensor_copy(
        sb["HALO"][:, :].rearrange("p (c w) -> p c w", w=3),
        BIGA[:, 0:NCH * CS].rearrange("p (c w) -> p c w", w=CS)[:, :, TPC:TPC + 3]),
        inc="dve")
    em.dma("gpsimd", dr["cc2i"], sb["HALO"][:, :], deps=[th], stream="cc")
    em.wait("gpsimd", ("dq_cc", em.count("dq_cc")))
    em.emit("gpsimd", lambda E: E.collective_compute(
        "AllGather", ALU.bypass, replica_groups=[list(range(NCORES))],
        ins=[dr["cc2i"]], outs=[dr["cc2o"]]), inc="cc2")
    tcc2 = ("cc2", 1)
    em.waits("vector", [tcc2] + list(ln_state["t1_free"]))
    tmz = em.emit("vector", lambda E: E.memset(sb["T2"][:, 0:48], 0.0), inc="dve")
    gfree = tmz
    for r in range(NCORES):
        tg = em.dma("sync", sb["DOWN"][:, 0:48], dr["cc2o"][r * 128:(r + 1) * 128, :],
                    deps=[gfree, tcc2], stream="ccg")
        em.waits("vector", [tg])
        gfree = em.emit("vector", lambda E, r=r: E.scalar_tensor_tensor(
            sb["T2"][:, 0:48], sb["DOWN"][:, 0:48], sb["prevh"][:, r:r + 1],
            sb["T2"][:, 0:48], ALU.mult, ALU.add), inc="dve")
    thal = em.emit("vector", lambda E: E.tensor_copy(
        BIGA[:, 0:NCH * CS].rearrange("p (c w) -> p c w", w=CS)[:, :, 0:3],
        sb["T2"][:, 0:48].rearrange("p (c w) -> p c w", w=3)), inc="dve")

    em.dma("gpsimd", sb["WZ"][:, 0:256], dr["wbg"].transpose([1, 0, 2]), stream="wfm")
    tbg = ("dq_wfm", em.count("dq_wfm"))
    WBG = sb["WZ"][:, 0:256]
    RSQ = sb["RSQ"]

    pq_free = [None, None, None, None]
    wl_free = {}
    spl_free = {}
    stat_free = None
    tgam_sp = None
    conv_mb_free = None
    u_toks = []
    last_u = None
    XC = lambda j: sb["XCG"][:, j * GRP:(j + 1) * GRP]
    for g in range(NG):
        conv_done = None
        for c in range(NCH):
            base = c * CS + 3 + g * GRP
            cb = sb["T1"][:, 0:GRP] if c % 2 == 0 else sb["MB"][:, 0:GRP]
            em.waits("vector", [thal, conv_mb_free, spl_free.get(("cv", c % 2))] + list(hbdep))
            em.emit("vector", lambda E, base=base, c=c, cb=cb: E.tensor_scalar(
                cb, BIGA[:, base - 3:base - 3 + GRP],
                sb["convw"][:, 4 * c:4 * c + 1], None, ALU.mult))
            tcv = None
            for kk in range(1, 4):
                tcv = em.emit("vector", lambda E, base=base, c=c, kk=kk, cb=cb: E.scalar_tensor_tensor(
                    cb, BIGA[:, base - 3 + kk:base - 3 + kk + GRP],
                    sb["convw"][:, 4 * c + kk:4 * c + kk + 1], cb,
                    ALU.mult, ALU.add), inc="dve" if kk == 3 else None)
            em.waits("scalar", [tcv, conv_done])
            tsi = em.emit("scalar", lambda E, c=c, cb=cb: E.activation(
                XC(c), cb, AF.Silu, bias=sb["convb"][:, c:c + 1]), inc="act")
            conv_done = tsi
            spl_free[("cv", c % 2)] = tsi
            if "dbg_xc" in dr:
                em.dma("gpsimd", dr["dbg_xc"][c * 128:(c + 1) * 128, g * GRP:(g + 1) * GRP],
                       XC(c), deps=[tsi], stream="dxc")
        if P3STOP == "conv":
            continue
        tgm = None
        for ts in range(4):
            em.waits("tensor", [conv_done, tbg, pq_free[0]])
            for j in range(NCH):
                tgm = em.emit("tensor", lambda E, j=j, ts=ts: E.matmul(
                    ps[0][:, ts * 16:(ts + 1) * 16],
                    XC(j)[:, ts * 128:(ts + 1) * 128], WBG[:, j * 16:(j + 1) * 16],
                    start=(j == 0), stop=(j == NCH - 1)),
                    inc="pe" if j == NCH - 1 else None)
        em.waits("vector", [tgm])
        tga = em.emit("vector", lambda E, g=g: E.tensor_add(
            sb["GAM"][:, g * 64:(g + 1) * 64].rearrange("p (s h) -> p s h", h=HB),
            ps[0][:, 0:64].rearrange("p (s h) -> p s h", h=HB),
            sb["bgrep"][:, :].unsqueeze(1).broadcast_to([128, 4, HB])), inc="dve")
        pq_free[0] = tga
        em.waits("scalar", [tga])
        tgs = em.emit("scalar", lambda E, g=g: E.activation(
            sb["GAM"][:, g * 64:(g + 1) * 64], sb["GAM"][:, g * 64:(g + 1) * 64],
            AF.Sigmoid), inc="act")
        for ts in range(4):
            em.waits("tensor", [tgs, tgam_sp])
            ttg = em.emit("tensor", lambda E, g=g, ts=ts: E.transpose(
                ps["K"][0:16, 256:384],
                sb["GAM"][:, g * 64 + ts * 16:g * 64 + (ts + 1) * 16], sb["iden"][:, :]),
                inc="pe")
            em.waits("vector", [ttg])
            tge = em.emit("vector", lambda E: E.tensor_copy(sb["GAMF"][0:16, 0:128],
                                                            ps["K"][0:16, 256:384]), inc="dve")
            tgam_sp = em.dma("sync", dr["gam_hbm"][:, (g * 4 + ts) * 128:(g * 4 + ts + 1) * 128],
                             sb["GAMF"][0:16, 0:128], deps=[tge], stream="gam")
        if P3STOP == "gam":
            continue
        banks = [ps[0], ps[1], ps[2], ps[3]]
        tst = None
        for oc in range(NCH):
            specs = [("wbq", dr["qn_sp"], "q"), ("wbk", dr["kraw_sp"], "k"),
                     ("wbv", dr["vraw_sp"], "v"), ("wbig", dr["igr_sp"], "ig"),
                     ("wbog", dr["og_sp"], "og")]
            if P3STOP == "p1qkv":
                specs = specs[:3]
            elif P3STOP in ("p1q", "p1mm", "p1sq", "p1sp"):
                specs = specs[:1]
            elif P3STOP == "p1ig":
                specs = specs[3:]
            for si, (wnm, spdst, role) in enumerate(specs):
                wsl = (oc * 5 + si) % 2
                wbuf = sb["WX"][:, wsl * 2048:(wsl + 1) * 2048]
                twl = em.dma("gpsimd", wbuf, blk3(dr[wnm][oc]),
                             deps=[wl_free.get(wsl)] + list(CT), stream=f"wb{wsl}")
                bank = banks[si % 4]
                em.waits("tensor", [twl, pq_free[si % 4], conv_done] + list(hbdep))
                tpm = None
                for j in range(NCH):
                    if role in ("ig", "og"):
                        rhs = XC(j)
                    else:
                        rhs = BIGA[:, dcol(j, g * GRP):dcol(j, (g + 1) * GRP)]
                    tpm = em.emit("tensor", lambda E, j=j, wbuf=wbuf, bank=bank, rhs=rhs: E.matmul(
                        bank[:, 0:GRP], wbuf[:, j * 128:(j + 1) * 128], rhs,
                        start=(j == 0), stop=(j == NCH - 1)),
                        inc="pe" if j == NCH - 1 else None)
                wl_free[wsl] = tpm
                if role in ("q", "k", "v"):
                    raw = sb["GS"][:, GRP:2 * GRP]
                    em.waits("vector", [tpm, spl_free.get("qsl"), spl_free.get("qsl2")])
                    trw = em.emit("vector", lambda E, raw=raw, bank=bank: E.tensor_copy(
                        raw, bank[:, 0:GRP]), inc="dve")
                    if P3STOP == "p1mm":
                        pq_free[si % 4] = trw
                        spl_free["qsl"] = trw
                        tst = trw
                        continue
                    if P3STOP == "p1sq":
                        sq = sb["XSQ"][:, 0:GRP]
                        em.waits("scalar", [tpm, spl_free.get("sq")])
                        tsq = em.emit("scalar", lambda E, sq=sq, bank=bank: E.activation(
                            sq, bank[:, 0:GRP], AF.Square), inc="act")
                        pq_free[si % 4] = tsq
                        spl_free["qsl"] = trw
                        spl_free["sq"] = tsq
                        tst = tsq
                        continue
                    if P3STOP == "p1sp":
                        tspr0 = em.dma("sync", spdst[oc, :, g * GRP:(g + 1) * GRP], raw,
                                       deps=[trw], stream="rsp")
                        pq_free[si % 4] = trw
                        spl_free["qsl"] = tspr0
                        tst = trw
                        continue
                    tspr = em.dma("sync", spdst[oc, :, g * GRP:(g + 1) * GRP], raw,
                                  deps=[trw], stream="rsp")
                    spl_free["qsl"] = tspr
                    sq = sb["XSQ"][:, 0:GRP]
                    em.waits("scalar", [trw, spl_free.get("sq")])
                    tsq = em.emit("scalar", lambda E, sq=sq, raw=raw: E.activation(
                        sq, raw, AF.Square), inc="act")
                    koff = {"q": 0, "k": 64, "v": 192}[role]
                    if P3STOP == "p1nostats":
                        pq_free[si % 4] = trw
                        spl_free["qsl2"] = trw
                        tst = tsq
                        continue
                    em.waits("tensor", [tsq, trw, stat_free])
                    for ts in range(4):
                        if role == "v":
                            em.emit("tensor", lambda E, ts=ts, oc=oc, raw=raw: E.matmul(
                                ps["K"][:, 128 + ts * 16 + oc:128 + ts * 16 + oc + 1],
                                raw[:, ts * 128:(ts + 1) * 128], sb["onescb"][:, 0:1],
                                start=True, stop=True))
                        tst = em.emit("tensor", lambda E, ts=ts, koff=koff, oc=oc, sq=sq: E.matmul(
                            ps["K"][:, koff + ts * 16 + oc:koff + ts * 16 + oc + 1],
                            sq[:, ts * 128:(ts + 1) * 128], sb["onescb"][:, 0:1],
                            start=True, stop=True), inc="pe" if ts == 3 else None)
                    pq_free[si % 4] = tst
                    spl_free["sq"] = tst
                    spl_free["qsl2"] = tst
                else:
                    gate = sb["GS"][:, GRP:2 * GRP]
                    bias = sb["bigc"] if role == "ig" else sb["bogc"]
                    em.waits("scalar", [tpm, spl_free.get("qsl"), spl_free.get("qsl2")])
                    tsg = em.emit("scalar", lambda E, gate=gate, bank=bank, bias=bias, oc=oc: E.activation(
                        gate, bank[:, 0:GRP], AF.Sigmoid, bias=bias[:, oc:oc + 1]), inc="act")
                    tspg = em.dma("sync", spdst[oc, :, g * GRP:(g + 1) * GRP], gate,
                                  deps=[tsg], stream="rsp")
                    spl_free["qsl"] = tspg
                    pq_free[si % 4] = tsg
        em.waits("vector", [tst, conv_done])
        em.emit("vector", lambda E: E.tensor_scalar(sb["T1"][:, 0:128], ps["K"][:, 0:128],
                                                    sb["epsc"][:, 1:2], None, ALU.add))
        em.emit("vector", lambda E: E.tensor_scalar_mul(sb["T2"][:, 0:64], ps["K"][:, 128:192], 1.0 / 128))
        tfz = em.emit("vector", lambda E: E.tensor_scalar_mul(sb["T2"][:, 64:128], ps["K"][:, 192:256], 1.0 / 128),
                      inc="dve")
        stat_free = tfz
        em.waits("scalar", [tfz])
        tsq1 = em.emit("scalar", lambda E: E.activation(sb["T1"][:, 0:128], sb["T1"][:, 0:128],
                                                        AF.Sqrt), inc="act")
        em.waits("vector", [tsq1])
        em.emit("vector", lambda E, g=g: E.reciprocal(RSQ[:, g * 64:(g + 1) * 64], sb["T1"][:, 0:64]))
        em.emit("vector", lambda E: E.reciprocal(sb["STT2"][:, 0:64], sb["T1"][:, 64:128]))
        em.emit("vector", lambda E: E.tensor_mul(sb["T1"][:, 0:64], sb["T2"][:, 0:64], sb["T2"][:, 0:64]))
        em.emit("vector", lambda E: E.tensor_sub(sb["T1"][:, 0:64], sb["T2"][:, 64:128], sb["T1"][:, 0:64]))
        em.emit("vector", lambda E: E.tensor_scalar_max(sb["T1"][:, 0:64], sb["T1"][:, 0:64], 0.0))
        tvv = em.emit("vector", lambda E: E.tensor_scalar(sb["T1"][:, 0:64], sb["T1"][:, 0:64],
                                                          sb["epsc"][:, 0:1], None, ALU.add), inc="dve")
        em.waits("scalar", [tvv])
        tsq2 = em.emit("scalar", lambda E: E.activation(sb["T1"][:, 64:128], sb["T1"][:, 0:64],
                                                        AF.Sqrt), inc="act")
        em.waits("vector", [tsq2])
        em.emit("vector", lambda E: E.reciprocal(sb["DOWN"][:, 0:64], sb["T1"][:, 64:128]))
        tab = em.emit("vector", lambda E: E.tensor_mul(sb["DOWN"][:, 64:128], sb["T2"][:, 0:64],
                                                       sb["DOWN"][:, 0:64]), inc="dve")
        tt_ab = None
        for kind, srcb in enumerate((sb["STT2"][:, 0:64], sb["DOWN"][:, 0:64], sb["DOWN"][:, 64:128])):
            for ts in range(4):
                em.waits("tensor", [tab, tt_ab])
                ttk = em.emit("tensor", lambda E, srcb=srcb, ts=ts: E.transpose(
                    ps["K"][0:16, 256:384], srcb[:, ts * 16:(ts + 1) * 16], sb["iden"][:, :]),
                    inc="pe")
                em.waits("vector", [ttk])
                tt_ab = em.emit("vector", lambda E, kind=kind, ts=ts: E.tensor_copy(
                    sb["AB2"][0:16, (kind * 4 + ts) * 128:(kind * 4 + ts + 1) * 128],
                    ps["K"][0:16, 256:384]), inc="dve")
        for oc in range(NCH):
            krr = sb["GS"][:, GRP:2 * GRP]
            vrr = sb["XSQ"][:, 0:GRP]
            igr = sb["GS"][:, 0:GRP]
            rdep = [spl_free.get("k2"), ("dq_rsp", em.count("dq_rsp"))]
            tk1 = em.dma("sync", krr, dr["kraw_sp"][oc, :, g * GRP:(g + 1) * GRP],
                         deps=rdep, stream="rr")
            tk2 = em.dma("sync", vrr, dr["vraw_sp"][oc, :, g * GRP:(g + 1) * GRP],
                         deps=rdep, stream="rr")
            tk3 = em.dma("sync", igr, dr["igr_sp"][oc, :, g * GRP:(g + 1) * GRP],
                         deps=rdep, stream="rr")
            em.waits("tensor", [tt_ab, spl_free.get("ubc")] + pq_free)
            tbc = None
            for kind in range(3):
                for ts in range(4):
                    tbc = em.emit("tensor", lambda E, kind=kind, ts=ts, oc=oc: E.matmul(
                        ps[kind][:, ts * 128:(ts + 1) * 128],
                        sb["oh16b"][:, oc * 128:(oc + 1) * 128],
                        sb["AB2"][0:16, (kind * 4 + ts) * 128:(kind * 4 + ts + 1) * 128],
                        start=True, stop=True),
                        inc="pe" if (kind == 2 and ts == 3) else None)
            em.waits("vector", [tbc, tk1, tk2, tk3, spl_free.get("u_t1")])
            em.emit("vector", lambda E, krr=krr: E.tensor_mul(sb["T1"][:, 0:GRP], krr, ps[0][:, 0:GRP]))
            em.emit("vector", lambda E, vrr=vrr: E.tensor_mul(sb["T2"][:, 0:GRP], vrr, ps[1][:, 0:GRP]))
            tvn = em.emit("vector", lambda E: E.tensor_sub(sb["T2"][:, 0:GRP], sb["T2"][:, 0:GRP],
                                                           ps[2][:, 0:GRP]), inc="dve")
            spl_free["ubc"] = tvn
            em.waits("vector", [spl_free.get("u_mb")])
            tvna = em.emit("vector", lambda E: E.tensor_scalar(
                sb["MB"][:, 0:GRP], sb["T2"][:, 0:GRP], sb["vnw"][:, 0:1], sb["vnb"][:, 0:1],
                ALU.mult, ALU.add), inc="dve")
            em.waits("gpsimd", [tvna, tk3, tvn])
            tu1 = em.emit("gpsimd", lambda E, igr=igr: E.tensor_mul(sb["T1"][:, 0:GRP],
                                                                    sb["T1"][:, 0:GRP], igr), inc="pool")
            spl_free["u_t1"] = tu1
            tu2 = em.emit("gpsimd", lambda E, oc=oc, g=g: E.tensor_mul(
                BIGB[:, dcol(oc, g * GRP):dcol(oc, (g + 1) * GRP)],
                sb["T1"][:, 0:GRP], sb["MB"][:, 0:GRP]), inc="pool")
            spl_free["k2"] = tu2
            spl_free["u_mb"] = tu2
            last_u = tu2
            u_toks.append(tu2)
        conv_mb_free = last_u

    if P3STOP in ("conv", "gam", "pass1", "p1nostats", "p1qkv", "p1q", "p1ig", "p1mm", "p1sq", "p1sp", "u"):
        return [("dq_yxn", em.count("dq_yxn")), ("dq_cp", em.count("dq_cp")),
                ("pool", em.count("pool")), ("dve", em.count("dve")),
                ("act", em.count("act")), ("pe", em.count("pe")),
                ("dq_rsp", em.count("dq_rsp")), ("dq_gam", em.count("dq_gam"))]
    if "dbg_u" in dr:
        for c in range(NCH):
            em.dma("gpsimd", dr["dbg_u"][c * 128:(c + 1) * 128, :],
                   BIGB[:, dcol(c, 0):dcol(c, TPC)], deps=list(u_toks[-3:]), stream="dxc")
    gb_free = None
    tab3 = None
    for h in range(HB):
        tgb = em.dma("sync", XF[:, 0:TPC], dr["gam_hbm"][h:h + 1, :].partition_broadcast(128).squeeze(1),
                     deps=[gb_free, ("dq_gam", em.count("dq_gam"))], stream="gb")
        em.waits("vector", [tgb, last_u])
        em.emit("vector", lambda E: E.tensor_tensor_scan(
            XF[:, TPC:2 * TPC], XF[:, 0:TPC], XF[:, 0:TPC], 1.0, ALU.mult, ALU.bypass))
        em.emit("vector", lambda E: E.tensor_copy(sb["T1"][:, 144:160], XF[:, TPC:TPC + 16]))
        em.emit("vector", lambda E, h=h: E.tensor_copy(sb["AB3"][:, h:h + 1], XF[:, 2 * TPC - 1:2 * TPC]))
        em.emit("vector", lambda E, h=h: E.tensor_tensor_scan(
            XF[:, TPC:2 * TPC], XF[:, 0:TPC], BIGB[:, dcol(h, 0):dcol(h, TPC)], 0.0,
            ALU.mult, ALU.add))
        em.emit("vector", lambda E: E.tensor_copy(sb["T1"][:, 160:176], XF[:, TPC:TPC + 16]))
        tab3 = em.emit("vector", lambda E, h=h: E.tensor_copy(sb["AB3"][:, 16 + h:17 + h],
                                                              XF[:, 2 * TPC - 1:2 * TPC]), inc="dve")
        gb_free = tab3
    em.dma("gpsimd", dr["cc3i"], sb["AB3"][:, 0:32], deps=[tab3], stream="cc")
    em.wait("gpsimd", ("dq_cc", em.count("dq_cc")))
    em.emit("gpsimd", lambda E: E.collective_compute(
        "AllGather", ALU.bypass, replica_groups=[list(range(NCORES))],
        ins=[dr["cc3i"]], outs=[dr["cc3o"]]), inc="cc3")
    tcc3 = ("cc3", 1)
    T1v = lambda r: sb["T1"][:, 0:144].rearrange("p (h k) -> p h k", k=9)[:, :, r + 1:r + 2].squeeze(2)
    T2v = lambda r: sb["T2"][:, 0:144].rearrange("p (h k) -> p h k", k=9)[:, :, r + 1:r + 2].squeeze(2)
    em.waits("vector", [tcc3])
    em.emit("vector", lambda E: E.memset(sb["T1"][:, 0:144], 0.0))
    em.emit("vector", lambda E: E.memset(sb["T2"][:, 0:144], 0.0))
    tdl = em.dma("sync",
                 sb["DOWN"][:, 0:256].rearrange("p (r c) -> p r c", r=NCORES),
                 dr["cc3o"].rearrange("(r p) c -> p r c", r=NCORES),
                 deps=[tcc3, tab3], stream="ccg")
    em.waits("vector", [tdl])
    for r in range(NCORES):
        em.emit("vector", lambda E, r=r: E.tensor_scalar(
            T1v(r), sb["DOWN"][:, r * 32:r * 32 + 16], -1.0, None, ALU.add))
    for r in range(NCORES):
        em.emit("vector", lambda E, r=r: E.tensor_scalar(
            T1v(r), T1v(r), sb["inc8"][:, r:r + 1], sb["epsc"][:, 2:3],
            ALU.mult, ALU.add))
    for r in range(NCORES):
        em.emit("vector", lambda E, r=r: E.tensor_scalar(
            T2v(r), sb["DOWN"][:, r * 32 + 16:r * 32 + 32],
            sb["inc8"][:, r:r + 1], None, ALU.mult))
    em.emit("vector", lambda E: E.tensor_copy(sb["T1"][:, 176:192], sb["T1"][:, 0:16]))
    em.emit("vector", lambda E: E.tensor_tensor_scan(
        sb["MB"][:, 0:144], sb["T1"][:, 0:144], sb["T2"][:, 0:144], 0.0,
        ALU.mult, ALU.add))
    em.emit("vector", lambda E: E.tensor_copy(sb["T1"][:, 192:208], sb["MB"][:, 0:16]))
    gfree = em.emit("vector", lambda E: E.tensor_copy(
        sb["AB3"][:, 32:48],
        sb["MB"][:, 0:144].rearrange("p (h k) -> p h k", k=9)[:, :, 8:9].squeeze(2)),
        inc="dve")

    if P3STOP == "locals":
        return [("dq_yxn", em.count("dq_yxn")), ("dq_cp", em.count("dq_cp")),
                ("dve", em.count("dve")), ("cc3", 1)]
    SCR2 = BIGA[:, 0:TPC]
    MEMB = BIGA[:, TPC:2 * TPC]
    QRW = BIGA[:, 2 * TPC:3 * TPC]
    OGR = BIGA[:, 3 * TPC:4 * TPC]
    ZB = BIGA[:, 4 * TPC:5 * TPC]
    ro = {}
    out_toks = []
    for h in range(HB):
        tgb = em.dma("sync", XF[:, 0:TPC], dr["gam_hbm"][h:h + 1, :].partition_broadcast(128).squeeze(1),
                     deps=[gfree, ro.get("gb"), ("dq_gb", em.count("dq_gb"))], stream="gb")
        tq1 = em.dma("sync", QRW, dr["qn_sp"][h],
                     deps=[gfree, last_u, ro.get("qrw"), ("dq_rr", em.count("dq_rr")),
                           ("dq_rsp", em.count("dq_rsp"))], stream="ro")
        tq2 = em.dma("sync", OGR, dr["og_sp"][h], deps=[gfree, last_u, ro.get("ogr")], stream="ro")
        em.waits("vector", [tgb, gfree])
        tsc3 = em.emit("vector", lambda E, h=h: E.tensor_tensor_scan(
            XF[:, TPC:2 * TPC], XF[:, 0:TPC], BIGB[:, dcol(h, 0):dcol(h, TPC)],
            sb["AB3"][:, 32 + h:33 + h], ALU.mult, ALU.add), inc="dve")
        ro["gb"] = tsc3
        if h == 0 and "dbg_mem" in dr:
            em.dma("gpsimd", dr["dbg_mem"], sb["XCG"][:, :].bitcast(F32)[:, TPC:2 * TPC],
                   deps=[tsc3], stream="dxc")
        em.waits("vector", [tsc3, ro.get("memb"), last_u if h == 0 else None])
        em.emit("vector", lambda E: E.tensor_copy(sb["T2"][:, 192:208], XF[:, TPC:TPC + 16]))
        tmb2 = em.emit("vector", lambda E: E.tensor_copy(MEMB, XF[:, TPC:2 * TPC]), inc="dve")
        em.waits("scalar", [tmb2, ro.get("scr")])
        tsq3 = em.emit("scalar", lambda E: E.activation(SCR2, MEMB, AF.Square), inc="act")
        em.waits("tensor", [tsq3, tmb2, ro.get("stat")])
        tst2 = None
        for ts in range(NTS):
            em.emit("tensor", lambda E, ts=ts: E.matmul(
                ps["K"][:, ts:ts + 1], MEMB[:, ts * 128:(ts + 1) * 128], sb["onescb"][:, 0:1],
                start=True, stop=True))
            tst2 = em.emit("tensor", lambda E, ts=ts: E.matmul(
                ps["K"][:, 16 + ts:17 + ts], SCR2[:, ts * 128:(ts + 1) * 128], sb["onescb"][:, 0:1],
                start=True, stop=True), inc="pe" if ts == NTS - 1 else None)
        if h == 0 and "dbg_pk" in dr:
            em.dma("gpsimd", dr["dbg_memb"], MEMB, deps=[tmb2], stream="dxc")
            em.dma("gpsimd", dr["dbg_scr"], SCR2, deps=[tsq3], stream="dxc")
        em.waits("vector", [tst2, ro.get("t12")])
        if h == 0 and P3STOP == "ro1":
            tpkd = em.emit("vector", lambda E: E.tensor_copy(sb["GS"][:, 0:64], ps["K"][:, 0:64]), inc="dve")
            em.dma("gpsimd", dr["dbg_pk"], sb["GS"][:, 0:64], deps=[tpkd], stream="dxc")
        em.emit("vector", lambda E: E.tensor_scalar_mul(sb["T1"][:, 0:NTS], ps["K"][:, 0:NTS], 1.0 / 128))
        em.emit("vector", lambda E: E.tensor_scalar_mul(sb["T1"][:, 64:64 + NTS], ps["K"][:, 16:16 + NTS], 1.0 / 128))
        em.emit("vector", lambda E: E.tensor_mul(sb["T2"][:, 0:NTS], sb["T1"][:, 0:NTS], sb["T1"][:, 0:NTS]))
        em.emit("vector", lambda E: E.tensor_copy(sb["T1"][:, 96:96 + NTS], sb["T1"][:, 0:NTS]))
        em.emit("vector", lambda E: E.tensor_sub(sb["T2"][:, 0:NTS], sb["T1"][:, 64:64 + NTS], sb["T2"][:, 0:NTS]))
        em.emit("vector", lambda E: E.tensor_copy(sb["T1"][:, 112:112 + NTS], sb["T1"][:, 64:64 + NTS]))
        tvar = em.emit("vector", lambda E: E.tensor_scalar(sb["T2"][:, 0:NTS], sb["T2"][:, 0:NTS],
                                                           sb["epsc"][:, 3:4], sb["epsc"][:, 0:1],
                                                           ALU.max, ALU.add), inc="dve")
        ro["stat"] = tvar
        em.waits("scalar", [tvar])
        tsq4 = em.emit("scalar", lambda E: E.activation(sb["T2"][:, 64:64 + NTS], sb["T2"][:, 0:NTS],
                                                        AF.Sqrt), inc="act")
        em.waits("vector", [tsq4])
        em.emit("vector", lambda E: E.reciprocal(sb["STT2"][:, 0:NTS], sb["T2"][:, 64:64 + NTS]))
        em.emit("vector", lambda E, h=h: E.tensor_copy(
            sb["STT2"][:, 32:32 + NTS],
            RSQ[:, 0:NTS * 16].rearrange("p (s o) -> p s o", o=16)[:, :, h:h + 1].squeeze(2)))
        tb2 = em.emit("vector", lambda E: E.tensor_mul(sb["STT2"][:, 16:16 + NTS], sb["T1"][:, 0:NTS],
                                                       sb["STT2"][:, 0:NTS]), inc="dve")
        tta = None
        for kind in range(3):
            em.waits("tensor", [tb2, tta])
            ttk2 = em.emit("tensor", lambda E, kind=kind: E.transpose(
                ps["K"][0:NTS, 256:384], sb["STT2"][:, kind * 16:kind * 16 + NTS], sb["iden"][:, :]),
                inc="pe")
            em.waits("vector", [ttk2])
            tta = em.emit("vector", lambda E, kind=kind: E.tensor_copy(
                sb["ABFM"][0:NTS, kind * 128:(kind + 1) * 128], ps["K"][0:NTS, 256:384]), inc="dve")
        if h == 0 and P3STOP == "ro1":
            em.dma("gpsimd", dr["dbg_st1"], sb["STT2"][:, 0:64], deps=[tta, tb2], stream="dxc")
            em.dma("gpsimd", dr["dbg_ab"][0:NTS, :], sb["ABFM"][0:NTS, :], deps=[tta], stream="dxc")
            return [tta, tb2, ("pool", em.count("pool")), ("act", em.count("act")),
                    ("pe", em.count("pe")), ("dve", em.count("dve"))]
        tz = None
        for qq in range(TPC // 512):
            em.waits("tensor", [tta, ro.get("zpsum")])
            tbc2 = None
            for ts in range(4):
                tsg = qq * 4 + ts
                em.emit("tensor", lambda E, ts=ts, tsg=tsg: E.matmul(
                    ps[0][:, ts * 128:(ts + 1) * 128],
                    sb["oh16b"][0:NTS, tsg * 128:(tsg + 1) * 128], sb["ABFM"][0:NTS, 0:128],
                    start=True, stop=True))
                em.emit("tensor", lambda E, ts=ts, tsg=tsg: E.matmul(
                    ps[1][:, ts * 128:(ts + 1) * 128],
                    sb["oh16b"][0:NTS, tsg * 128:(tsg + 1) * 128], sb["ABFM"][0:NTS, 128:256],
                    start=True, stop=True))
                tbc2 = em.emit("tensor", lambda E, ts=ts, tsg=tsg: E.matmul(
                    ps[2][:, ts * 128:(ts + 1) * 128],
                    sb["oh16b"][0:NTS, tsg * 128:(tsg + 1) * 128], sb["ABFM"][0:NTS, 256:384],
                    start=True, stop=True), inc="pe" if ts == 3 else None)
            em.waits("vector", [tbc2, tq1, ro.get("t12b")])
            em.emit("vector", lambda E, qq=qq: E.tensor_mul(
                sb["T1"][:, 0:GRP], XF[:, TPC + qq * GRP:TPC + (qq + 1) * GRP], ps[0][:, 0:GRP]))
            em.emit("vector", lambda E: E.tensor_sub(sb["T1"][:, 0:GRP], sb["T1"][:, 0:GRP],
                                                     ps[1][:, 0:GRP]))
            tqn = em.emit("vector", lambda E, qq=qq: E.tensor_mul(
                sb["T2"][:, 0:GRP], QRW[:, qq * GRP:(qq + 1) * GRP],
                ps[2][:, 0:GRP]), inc="dve")
            ro["zpsum"] = tqn
            em.waits("vector", [ro.get("zmb")])
            tza = em.emit("vector", lambda E: E.tensor_scalar(
                sb["MB"][:, 0:GRP], sb["T1"][:, 0:GRP], sb["mnw"][:, 0:1], sb["mnb"][:, 0:1],
                ALU.mult, ALU.add), inc="dve")
            em.waits("gpsimd", [tza, tqn])
            tz = em.emit("gpsimd", lambda E, qq=qq: E.tensor_mul(
                ZB[:, qq * GRP:(qq + 1) * GRP], sb["MB"][:, 0:GRP], sb["T2"][:, 0:GRP]),
                inc="pool")
            ro["t12b"] = tz
            ro["zmb"] = tz
        if h == 0 and "dbg_z" in dr:
            em.dma("gpsimd", dr["dbg_z"], ZB, deps=[tz], stream="dxc")
        em.waits("scalar", [tz, ro.get("scr")])
        tsq5 = em.emit("scalar", lambda E: E.activation(SCR2, ZB, AF.Square), inc="act")
        ro["scr"] = tsq5
        em.waits("tensor", [tsq5, tz, ro.get("stat2")])
        tst3 = None
        for ts in range(NTS):
            em.emit("tensor", lambda E, ts=ts: E.matmul(
                ps["K"][:, 32 + ts:33 + ts], ZB[:, ts * 128:(ts + 1) * 128], sb["onescb"][:, 0:1],
                start=True, stop=True))
            tst3 = em.emit("tensor", lambda E, ts=ts: E.matmul(
                ps["K"][:, 48 + ts:49 + ts], SCR2[:, ts * 128:(ts + 1) * 128], sb["onescb"][:, 0:1],
                start=True, stop=True), inc="pe" if ts == NTS - 1 else None)
        em.waits("vector", [tst3])
        em.emit("vector", lambda E: E.tensor_scalar_mul(sb["T1"][:, 0:NTS], ps["K"][:, 32:32 + NTS], 1.0 / 128))
        em.emit("vector", lambda E: E.tensor_scalar_mul(sb["T1"][:, 64:64 + NTS], ps["K"][:, 48:48 + NTS], 1.0 / 128))
        em.emit("vector", lambda E: E.tensor_mul(sb["T2"][:, 0:NTS], sb["T1"][:, 0:NTS], sb["T1"][:, 0:NTS]))
        em.emit("vector", lambda E: E.tensor_copy(sb["T1"][:, 96:96 + NTS], sb["T1"][:, 0:NTS]))
        em.emit("vector", lambda E: E.tensor_sub(sb["T2"][:, 0:NTS], sb["T1"][:, 64:64 + NTS], sb["T2"][:, 0:NTS]))
        em.emit("vector", lambda E: E.tensor_copy(sb["T1"][:, 112:112 + NTS], sb["T1"][:, 64:64 + NTS]))
        tvar2 = em.emit("vector", lambda E: E.tensor_scalar(sb["T2"][:, 0:NTS], sb["T2"][:, 0:NTS],
                                                            sb["epsc"][:, 3:4], sb["epsc"][:, 0:1],
                                                            ALU.max, ALU.add), inc="dve")
        ro["stat2"] = tvar2
        em.waits("scalar", [tvar2])
        tsq6 = em.emit("scalar", lambda E: E.activation(sb["T2"][:, 64:64 + NTS], sb["T2"][:, 0:NTS],
                                                        AF.Sqrt), inc="act")
        em.waits("vector", [tsq6])
        em.emit("vector", lambda E: E.reciprocal(sb["STT2"][:, 0:NTS], sb["T2"][:, 64:64 + NTS]))
        em.emit("vector", lambda E: E.tensor_copy(sb["T1"][:, 128:128 + NTS], sb["T1"][:, 0:NTS]))
        tb3 = em.emit("vector", lambda E: E.tensor_mul(sb["STT2"][:, 16:16 + NTS], sb["T1"][:, 0:NTS],
                                                       sb["STT2"][:, 0:NTS]), inc="dve")
        if h == 0 and "dbg_st2" in dr:
            em.dma("gpsimd", dr["dbg_st2"], sb["STT2"][:, 0:64], deps=[tb3], stream="dxc")
        tta2 = None
        for kind in range(2):
            em.waits("tensor", [tb3, tta2])
            ttk3 = em.emit("tensor", lambda E, kind=kind: E.transpose(
                ps["K"][0:NTS, 256:384], sb["STT2"][:, kind * 16:kind * 16 + NTS], sb["iden"][:, :]),
                inc="pe")
            em.waits("vector", [ttk3])
            tta2 = em.emit("vector", lambda E, kind=kind: E.tensor_copy(
                sb["ABFM"][0:NTS, kind * 128:(kind + 1) * 128], ps["K"][0:NTS, 256:384]), inc="dve")
        tout = None
        for qq in range(TPC // 512):
            em.waits("tensor", [tta2, ro.get("zpsum")])
            tbc3 = None
            for ts in range(4):
                tsg = qq * 4 + ts
                em.emit("tensor", lambda E, ts=ts, tsg=tsg: E.matmul(
                    ps[0][:, ts * 128:(ts + 1) * 128],
                    sb["oh16b"][0:NTS, tsg * 128:(tsg + 1) * 128], sb["ABFM"][0:NTS, 0:128],
                    start=True, stop=True))
                tbc3 = em.emit("tensor", lambda E, ts=ts, tsg=tsg: E.matmul(
                    ps[1][:, ts * 128:(ts + 1) * 128],
                    sb["oh16b"][0:NTS, tsg * 128:(tsg + 1) * 128], sb["ABFM"][0:NTS, 128:256],
                    start=True, stop=True), inc="pe" if ts == 3 else None)
            em.waits("vector", [tbc3, tq2, ro.get("t12")])
            em.emit("vector", lambda E, qq=qq: E.tensor_mul(
                sb["T1"][:, 0:GRP], ZB[:, qq * GRP:(qq + 1) * GRP], ps[0][:, 0:GRP]))
            tzs = em.emit("vector", lambda E: E.tensor_sub(sb["T1"][:, 0:GRP], sb["T1"][:, 0:GRP],
                                                           ps[1][:, 0:GRP]), inc="dve")
            ro["zpsum"] = tzs
            ro["t12"] = tzs
            em.waits("vector", [ro.get("zmb")])
            tza2 = em.emit("vector", lambda E, h=h: E.tensor_scalar(
                sb["MB"][:, 0:GRP], sb["T1"][:, 0:GRP], sb["gnw"][:, h:h + 1],
                sb["gnb"][:, h:h + 1], ALU.mult, ALU.add), inc="dve")
            em.waits("gpsimd", [tza2, tq2, tz])
            tout = em.emit("gpsimd", lambda E, h=h, qq=qq: E.tensor_mul(
                BIGB[:, dcol(h, qq * GRP):dcol(h, (qq + 1) * GRP)],
                sb["MB"][:, 0:GRP], OGR[:, qq * GRP:(qq + 1) * GRP]), inc="pool")
            ro["zmb"] = tout
        ro["qrw"] = tout
        ro["ogr"] = tout
        ro["memb"] = tout
        out_toks.append(tout)

    if "dbg_outmix" in dr:
        for c in range(NCH):
            em.dma("gpsimd", dr["dbg_outmix"][c * 128:(c + 1) * 128, :],
                   BIGB[:, dcol(c, 0):dcol(c, TPC)], deps=list(out_toks), stream="dxc")
    wo_free = [None, None]
    pw_free = [None, None]
    dn3_free = [None, None]
    outdep = tuple(out_toks)
    for g in range(NG):
        for oc in range(NCH):
            ws2 = oc % 2
            wo = sb["WX"][:, ws2 * 2048:(ws2 + 1) * 2048]
            two = em.dma("gpsimd", wo, blk3(dr["wbo"][oc]),
                         deps=[wo_free[ws2], ("dq_wb0", em.count("dq_wb0")),
                               ("dq_wb1", em.count("dq_wb1"))], stream=f"wc{ws2}")
            pb2 = ps[0] if ws2 == 0 else ps[1]
            em.waits("tensor", [two, pw_free[ws2]] + list(outdep))
            tom = None
            for j in range(NCH):
                tom = em.emit("tensor", lambda E, j=j, wo=wo, pb2=pb2, g=g: E.matmul(
                    pb2[:, 0:GRP], wo[:, j * 128:(j + 1) * 128],
                    BIGB[:, dcol(j, g * GRP):dcol(j, (g + 1) * GRP)],
                    start=(j == 0), stop=(j == NCH - 1)),
                    inc="pe" if j == NCH - 1 else None)
            wo_free[ws2] = tom
            dslot = sb["DOWN"][:, ws2 * GRP:(ws2 + 1) * GRP]
            em.waits("scalar", [tom, dn3_free[ws2]])
            tev4 = em.emit("scalar", lambda E, dslot=dslot, pb2=pb2: E.copy(dslot, pb2[:, 0:GRP]),
                           inc="act")
            pw_free[ws2] = tev4
            tac = em.dma("gpsimd", dr["y"][oc * 128:(oc + 1) * 128, g * GRP:(g + 1) * GRP],
                         dslot, deps=[tev4, t_cp2, txn_done], accum_op=ALU.add, stream="yacc")
            tac2 = em.dma("gpsimd", dr["xb1"][oc * 128:(oc + 1) * 128, g * GRP:(g + 1) * GRP],
                          dslot, deps=[tev4, txn_done], accum_op=ALU.add, stream="yacc")
            dn3_free[ws2] = tac2
    wo_done = [("dq_yacc", em.count("dq_yacc"))]

    acc = ffn_phase(dr["xb1"], dr["y"],
                    gate_blk=lambda ic: blk3(dr["wgate"][ic]),
                    up_blk=lambda ic: blk3(dr["wval"][ic]),
                    down_blk=lambda hc, p: blk3(dr["wout"][hc, p * NCH:(p + 1) * NCH]),
                    ln_kind="affine", wcol=sb["ln2w"], bcol=sb["ln2b"],
                    pre_dep=tuple(wo_done), acc_pre=tuple(wo_done))
    return acc


_PHASES = ("p1", "p2", "p3", "p4")
_DEBUG = ()


def kernel(**inputs):
    x = np.asarray(inputs["L0_attn_a_hidden_states"], np.float32)
    B, L, _ = x.shape
    TPC = L // (NCORES // B)
    shared, cores = host_prepare(inputs, TPC)
    nc = build_program(TPC, phases=_PHASES, debug=_DEBUG)
    in_maps = []
    for c in range(NCORES):
        m = dict(shared)
        m.update(cores[c])
        in_maps.append(m)
    res = run_bass_kernel_spmd(nc, in_maps, core_ids=list(range(NCORES)))
    QC = NCORES // B
    out = np.empty((B, L, D), np.float32)
    for c in range(NCORES):
        b, q = divmod(c, QC)
        out[b, q * TPC:(q + 1) * TPC, :] = np.asarray(res.results[c]["y"]).T
    return out
